# version 24
# baseline (speedup 1.0000x reference)
import numpy as np
import ml_dtypes
from contextlib import ExitStack

import concourse.bass as bass
import concourse.mybir as mybir
from concourse.alu_op_type import AluOpType as OP
from concourse.bass_utils import run_bass_kernel_spmd

F32 = mybir.dt.float32
BF16 = mybir.dt.bfloat16
AF = mybir.ActivationFunctionType
AX = mybir.AxisListType
BFNP = ml_dtypes.bfloat16

N_CORES = 8


class CFG:
    def __init__(self, dm=1024, L=2048, nb=4):
        self.dm = dm
        self.L = L
        self.nb = nb
        self.dI = 2 * dm
        self.ds = 16
        self.dtr = dm // 16
        self.xrows = self.dtr + 2 * self.ds
        self.ndch = self.dI // 128
        self.nkd = dm // 128
        self.tc = L // 4
        self.NT = min(512, self.tc)
        self.nnt = self.tc // self.NT
        self.ntq = L // 128
        self.ntq_tc = self.tc // 128
        self.tcd = L // self.tc



ENGINES = ("T", "V", "S", "G", "Y")


class Buf:
    __slots__ = ("name", "w", "readers", "dma_key")

    def __init__(self, name):
        self.name = name
        self.w = None
        self.readers = []
        self.dma_key = {}


class POp:
    __slots__ = ("eng", "fn", "waits", "tok", "dma_key", "inc", "pre_drain")

    def __init__(self, eng, fn, waits, tok, dma_key):
        self.eng = eng
        self.fn = fn
        self.waits = waits
        self.tok = tok
        self.dma_key = dma_key
        self.inc = False
        self.pre_drain = False


class Plan:
    def __init__(self):
        self.eng_ops = {e: [] for e in ENGINES}
        self.eng_n = {e: 0 for e in ENGINES}
        self.waited = {e: {} for e in ENGINES}
        self.dma_count = {}
        self.awaited = set()
        self.n_dma_keys = 0
        self.drained = {e: 0 for e in ENGINES}

    def buf(self, name):
        return Buf(name)

    def _dma_key_for(self, b, qcls):
        if qcls not in b.dma_key:
            b.dma_key[qcls] = ("D", self.n_dma_keys, b.name + "_" + qcls)
            self.n_dma_keys += 1
            self.dma_count[b.dma_key[qcls]] = 0
        return b.dma_key[qcls]

    def _waits(self, eng, reads, writes, accum=()):
        deps = []
        for b in reads:
            if b.w is not None:
                deps.append(b.w)
        for b in writes:
            if b.w is not None:
                deps.append(b.w)
            deps.extend(b.readers)
        for b in accum:
            if b.w is not None and b.w[0] != "T":
                deps.append(b.w)
            deps.extend(b.readers)
        out = []
        wmap = self.waited[eng]
        for (sk, val) in deps:
            cur = wmap.get(sk, 0)
            if val <= cur:
                continue
            wmap[sk] = val
            out.append((sk, val))
            if sk in self.eng_ops:
                self.awaited.add((sk, val))
        return out

    def barrier(self, eng, bufs):
        deps = []
        for b in bufs:
            if b.w is not None:
                deps.append(b.w)
            deps.extend(b.readers)
        waits = []
        wmap = self.waited[eng]
        for (sk, val) in deps:
            cur = wmap.get(sk, 0)
            if val <= cur:
                continue
            wmap[sk] = val
            waits.append((sk, val))
            if sk in self.eng_ops:
                self.awaited.add((sk, val))
        self.eng_n[eng] += 1
        o = POp(eng, None, waits, (eng, self.eng_n[eng]), None)
        self.eng_ops[eng].append(o)
        return o

    def op(self, eng, fn, reads=(), writes=(), accum=()):
        waits = self._waits(eng, reads, writes, accum)
        self.eng_n[eng] += 1
        tok = (eng, self.eng_n[eng])
        o = POp(eng, fn, waits, tok, None)
        self.eng_ops[eng].append(o)
        for b in writes:
            b.w = tok
            b.readers = []
        for b in accum:
            b.w = tok
            b.readers = []
        for b in reads:
            b.readers.append(tok)
        return o

    def dma(self, qeng, fn, reads=(), writes=(), sem_buf=None):
        waits = self._waits(qeng, reads, writes)
        key = self._dma_key_for(sem_buf, "G" if qeng == "G" else "H")
        self.dma_count[key] += 16
        tok = (key, self.dma_count[key])
        o = POp(qeng, fn, waits, tok, key)
        self.eng_ops[qeng].append(o)
        for b in writes:
            b.w = tok
            b.readers = []
        for b in reads:
            b.readers.append(tok)
        return o

    def finalize(self):
        rank = {}
        for e in ENGINES:
            ords = sorted(v for (ee, v) in self.awaited if ee == e)
            r = {}
            for i, v in enumerate(ords):
                r[v] = i + 1
            rank[e] = r
            for o in self.eng_ops[e]:
                if (e, o.tok[1]) in self.awaited:
                    o.inc = True
        for e in ENGINES:
            for o in self.eng_ops[e]:
                o.waits = [
                    (sk, rank[sk][val] if sk in self.eng_ops else val)
                    for (sk, val) in o.waits
                ]
        return rank




def build(cfg):
    c = cfg
    nc = bass.Bass("TRN2", target_bir_lowering=False)
    pl = Plan()

    DT = nc.dram_tensor
    x_in = DT("x", [c.nb, c.L, c.dm], F32, kind="ExternalInput").ap()
    out_d = DT("out", [c.nb, c.L, c.dm], F32, kind="ExternalOutput").ap()
    w_inT = DT("w_inT", [c.dm, 2 * c.dI], F32, kind="ExternalInput").ap()
    w_outT = DT("w_outT", [c.dI, c.dm], F32, kind="ExternalInput").ap()
    wxT = [DT(f"wxT_{d}", [c.dI, c.xrows], F32, kind="ExternalInput").ap() for d in range(2)]
    wdtT = [DT(f"wdtT_{d}", [c.dtr, c.dI], F32, kind="ExternalInput").ap() for d in range(2)]
    convw_d = [DT(f"convw_{d}", [128, 4 * c.ndch], F32, kind="ExternalInput").ap() for d in range(2)]
    convb_d = [DT(f"convb_{d}", [128, c.ndch], F32, kind="ExternalInput").ap() for d in range(2)]
    bdt_d = [DT(f"bdt_{d}", [128, c.ndch], F32, kind="ExternalInput").ap() for d in range(2)]
    Dd_d = [DT(f"Dvec_{d}", [128, c.ndch], F32, kind="ExternalInput").ap() for d in range(2)]
    alog_d = [DT(f"alog_{d}", [128, c.ndch * c.ds], F32, kind="ExternalInput").ap() for d in range(2)]
    basewT_d = DT("basewT", [c.dm, c.dm], F32, kind="ExternalInput").ap()
    baseb_d = DT("baseb", [1, c.dm], F32, kind="ExternalInput").ap()
    laA_d = DT("laA", [c.dm, 128], F32, kind="ExternalInput").ap()
    laB_d = DT("laB", [128, c.dm], F32, kind="ExternalInput").ap()
    rw_d = DT("rw", [c.dm, 8], F32, kind="ExternalInput").ap()
    rb_d = DT("rb", [1, 8], F32, kind="ExternalInput").ap()
    ident_d = DT("ident", [128, 128], BF16, kind="ExternalInput").ap()
    ones1_d = DT("ones1", [1, 128], F32, kind="ExternalInput").ap()

    xp_scr = DT("xp_scr", [c.nb, c.dI, c.L], BF16, kind="Internal").ap()
    sz_scr = DT("sz_scr", [c.nb, c.dI, c.L], BF16, kind="Internal").ap()
    u_scr = DT("u_scr", [2, c.dI, c.tc], BF16, kind="Internal").ap()
    p_scr = DT("p_scr", [c.nb, c.L, c.dm], F32, kind="Internal").ap()
    w_inT_bf = DT("w_inT_bf", [c.dm, 2 * c.dI], BF16, kind="Internal").ap()
    xdbl_scr = DT("xdbl_scr", [c.xrows, c.tc], BF16, kind="Internal").ap()
    w_outT_bf = DT("w_outT_bf", [c.dI, c.dm], BF16, kind="Internal").ap()
    basewT_bf = DT("basewT_bf", [c.dm, c.dm], BF16, kind="Internal").ap()

    xp_b = [[pl.buf(f"xp{b}_{k}") for k in range(c.ndch)] for b in range(c.nb)]
    sz_b = [[pl.buf(f"sz{b}_{k}") for k in range(c.ndch)] for b in range(c.nb)]
    u_b = [[pl.buf(f"u{p}_{k}") for k in range(c.ndch)] for p in range(2)]
    p_b = [[pl.buf(f"p{b}_{q}") for q in range(c.ntq)] for b in range(c.nb)]
    winbf_b = pl.buf("winbf")
    xdbl_scr_b = pl.buf("xdbl_scr")
    woutbf_b = pl.buf("woutbf")
    basebf_b = pl.buf("basebf")

    stk = ExitStack()
    SB = lambda name, shp, dt: stk.enter_context(nc.sbuf_tensor(name, shp, dt))
    PS = lambda name, shp, dt: stk.enter_context(nc.psum_tensor(name, shp, dt))

    class Tile:
        def __init__(self, name, shp, dt, psum=False):
            name = ("ps_" if psum else "sb_") + name
            self.t = PS(name, shp, dt) if psum else SB(name, shp, dt)
            self.b = pl.buf(name)

        def __getitem__(self, sl):
            return self.t[sl]

        @property
        def ap(self):
            return self.t[:]

    arn = [Tile(f"arn{i}", [128, c.tc], BF16) for i in range(32)]
    hfm = [Tile(f"hfm{i}", [128, c.tc], BF16) for i in range(4 * c.nkd)]
    wtile = [Tile(f"wt{i}", [128, 128], BF16) for i in range(2 * c.nkd)]
    yg = [Tile(f"yg{i}", [128, c.tc], BF16) for i in range(c.ndch)]
    wx_s = [[Tile(f"wx{d}_{k}", [128, c.xrows], BF16) for k in range(c.ndch)] for d in range(2)]
    wdt_s = [Tile(f"wdt{d}", [c.dtr, c.dI], BF16) for d in range(2)]
    convw_s = [Tile(f"convw{d}", [128, 4 * c.ndch], F32) for d in range(2)]
    convb_s = [Tile(f"convb{d}", [128, c.ndch], F32) for d in range(2)]
    bdt_s = [Tile(f"bdt{d}", [128, c.ndch], F32) for d in range(2)]
    Dd_s = [Tile(f"Dd{d}", [128, c.ndch], F32) for d in range(2)]
    negA_s = [Tile(f"negA{d}", [128, c.ndch * c.ds], F32) for d in range(2)]
    laA_s = [Tile(f"laA{k}", [128, 136], BF16) for k in range(c.nkd)]
    laB_s = Tile("laB", [128, c.dm], BF16)
    rb_s = Tile("rb", [128, 8], F32)
    bb_s = Tile("bb", [1, c.dm], F32)
    ones_s = Tile("ones", [1, 128], F32)
    ident_s = Tile("identt", [128, 128], BF16)

    x_sl = [Tile(f"x_sl{i}", [128, c.dm], F32) for i in range(2)]
    h1_sl = [Tile(f"h1_sl{i}", [128, c.dm], BF16) for i in range(2)]
    ev_bf = [Tile(f"ev_bf{i}", [128, c.NT], BF16) for i in range(2)]
    xp_sl = [Tile(f"xp_sl{i}", [128, c.tc + 6], BF16) for i in range(5)]
    acc_f = [Tile(f"acc_f{i}", [128, c.tc], F32) for i in range(2)]
    ca = [Tile(f"ca{i}", [128, c.tc], BF16) for i in range(2)]
    chA = [Tile(f"chA{i}", [128, c.tc], BF16) for i in range(2)]
    chB = [Tile(f"chB{i}", [128, c.tc], BF16) for i in range(2)]
    ubf = [Tile(f"ubf{i}", [128, c.tc], BF16) for i in range(2)]
    u_sl = [Tile(f"u_sl{i}", [128, c.tc], BF16) for i in range(3)]
    sz_sl = [Tile(f"sz_sl{i}", [128, c.tc], BF16) for i in range(2)]
    xdbl = Tile("xdbl", [max(c.xrows, 1), c.tc], BF16)
    dtb = [Tile(f"dtb{i}", [128, c.tc], BF16) for i in range(2)]
    dtu = [Tile(f"dtu{i}", [128, c.tc], BF16) for i in range(2)]
    dA_t = [Tile(f"dA{i}", [128, c.tc], BF16) for i in range(3)]
    dBu = [Tile(f"dBu{i}", [128, c.tc], BF16) for i in range(3)]
    hh = [Tile(f"hh{i}", [128, c.tc], BF16) for i in range(3)]
    hcc = [Tile(f"hcc{i}", [128, c.tc], BF16) for i in range(2)]
    carry = [Tile(f"carry{k}", [128, c.ds], BF16) for k in range(c.ndch)]
    wout_sl = [Tile(f"wout_sl{i}", [128, c.dm], BF16) for i in range(4)]
    pfld = [Tile(f"pfld{i}", [128, c.dm], F32) for i in range(2)]
    pstg = [Tile(f"pstg{i}", [128, c.dm], F32) for i in range(1)]
    out1t = [Tile(f"out1t{i}", [128, c.dm], F32) for i in range(2)]
    h2_sl = [Tile(f"h2_sl{i}", [128, c.dm], BF16) for i in range(2)]
    h2T = [Tile(f"h2T{k}", [128, 128], BF16) for k in range(c.nkd)]
    bw_sl = [Tile(f"bw_sl{i}", [128, c.dm], BF16) for i in range(2)]
    trash = Tile("trash", [128, c.dm], BF16)
    stat = Tile("stat", [128, 8], F32)
    cst = Tile("cst", [128, 4], F32)
    st2 = Tile("st2", [128, 8], F32)
    l8 = Tile("l8", [128, 8], F32)
    e8 = Tile("e8", [128, 8], F32)
    w2b = Tile("w2b", [128, 8], BF16)
    axb = Tile("axb", [128, 128], BF16)
    axw = Tile("axw", [128, 128], BF16)
    axwT = Tile("axwT", [128, 128], BF16)
    outst = [Tile(f"outst{i}", [128, c.NT], F32) for i in range(2)]

    P = [Tile(f"P{i}", [128, c.NT], F32, psum=True) for i in range(6)]
    PT = Tile("PTb", [128, 128], BF16, psum=True)
    PAX = Tile("PAXb", [128, 136], F32, psum=True)

    def dma(q, dst_ap, src_ap, reads, writes, sem_buf):
        pl.dma(q, lambda E, d=dst_ap, s=src_ap: E.dma_start(out=d, in_=s),
               reads=reads, writes=writes, sem_buf=sem_buf)

    def vop(fn, reads=(), writes=(), eng="V", accum=()):
        pl.op(eng, fn, reads=reads, writes=writes, accum=accum)

    for ci, cv in enumerate((1.0, -1.0, -0.5, 1e-5)):
        vop(lambda E, o=cst[:, ci:ci + 1], v=cv: E.memset(o, v), [], [cst.b])

    def convert_weight(src_ap, dst_ap, rows, cols, dst_buf):
        nrch = rows // 128
        ncch = (cols + c.tc - 1) // c.tc
        for r in range(nrch):
            for cc in range(ncch):
                w = min(c.tc, cols - cc * c.tc)
                fst = pfld[(r * ncch + cc) % 2]
                bst = u_sl[(r * ncch + cc) % 2]
                for off in range(0, w, c.dm):
                    ww = min(c.dm, w - off)
                    dma("Y", fst[:, 0:ww],
                        src_ap[r * 128:(r + 1) * 128, cc * c.tc + off:cc * c.tc + off + ww],
                        [], [fst.b], fst.b)
                    vop(lambda E, a=bst[:, off:off + ww], b_=fst[:, 0:ww]:
                        E.tensor_copy(a, b_), [fst.b], [bst.b])
                dma("Y", dst_ap[r * 128:(r + 1) * 128, cc * c.tc:cc * c.tc + w],
                    bst[:, 0:w], [bst.b], [dst_buf], bst.b)

    convert_weight(w_inT, w_inT_bf, c.dm, 2 * c.dI, winbf_b)
    convert_weight(w_outT, w_outT_bf, c.dI, c.dm, woutbf_b)
    convert_weight(basewT_d, basewT_bf, c.dm, c.dm, basebf_b)

    def load_small_f32(tile, src_ap):
        dma("Y", tile.ap, src_ap, [], [tile.b], tile.b)

    def load_small_bf16(tile, src_ap, rows, cols):
        for off in range(0, cols, c.dm):
            w = min(c.dm, cols - off)
            fst = pfld[(off // c.dm) % 2]
            dma("Y", fst[0:rows, 0:w], src_ap[:, off:off + w], [], [fst.b], fst.b)
            vop(lambda E, a=tile[0:rows, off:off + w], b_=fst[0:rows, 0:w]:
                E.tensor_copy(a, b_), [fst.b], [tile.b])

    for d in range(2):
        for k in range(c.ndch):
            load_small_bf16(wx_s[d][k], wxT[d][k * 128:(k + 1) * 128, :], 128, c.xrows)
        load_small_bf16(wdt_s[d], wdtT[d], c.dtr, c.dI)
        load_small_f32(convw_s[d], convw_d[d])
        load_small_f32(convb_s[d], convb_d[d])
        load_small_f32(bdt_s[d], bdt_d[d])
        load_small_f32(Dd_s[d], Dd_d[d])
        fst = pfld[1]
        dma("Y", fst[:, 0:c.ndch * c.ds], alog_d[d], [], [fst.b], fst.b)
        vop(lambda E, o=negA_s[d].ap, i=fst[:, 0:c.ndch * c.ds]:
            E.activation(o, i, AF.Exp), [fst.b], [negA_s[d].b], eng="S")
        vop(lambda E, o=negA_s[d].ap: E.tensor_scalar(o, o, -1.0, None, op0=OP.mult),
            [negA_s[d].b], [negA_s[d].b])
    for k in range(c.nkd):
        load_small_bf16(laA_s[k], laA_d[k * 128:(k + 1) * 128, :], 128, 128)
        fst2 = pfld[0]
        dma("Y", fst2[0:128, 0:8], rw_d[k * 128:(k + 1) * 128, :], [], [fst2.b], fst2.b)
        vop(lambda E, a=laA_s[k][:, 128:136], b_=fst2[0:128, 0:8]:
            E.tensor_copy(a, b_), [fst2.b], [laA_s[k].b])
    load_small_bf16(laB_s, laB_d, 128, c.dm)
    dma("Y", rb_s.ap, rb_d.partition_broadcast(128).squeeze(1), [], [rb_s.b], rb_s.b)
    load_small_f32(bb_s, baseb_d)
    load_small_f32(ones_s, ones1_d)
    dma("Y", ident_s.ap, ident_d, [], [ident_s.b], ident_s.b)

    def layer_norm(src, dst_bf, sq_col):
        vop(lambda E, o=stat[:, 0:1], i=src.ap: E.reduce_sum(o, i, axis=AX.X),
            [src.b], [stat.b])
        vop(lambda E, o=stat[:, 1:2], i=stat[:, 0:1]:
            E.tensor_scalar(o, i, 1.0 / c.dm, None, op0=OP.mult),
            [stat.b], [stat.b])
        vop(lambda E, o=trash.ap, i=src.ap, a=stat[:, 2:3]:
            E.activation(o, i, AF.Square, accum_out=a),
            [src.b], [trash.b, stat.b], eng="S")
        vop(lambda E, o=stat[:, 3:4], i=stat[:, 2:3]:
            E.tensor_scalar(o, i, 1.0 / c.dm, None, op0=OP.mult),
            [stat.b], [stat.b])
        vop(lambda E, o=stat[:, 4:5], a=stat[:, 1:2]:
            E.tensor_tensor(o, a, a, op=OP.mult), [stat.b], [stat.b])
        vop(lambda E, o=stat[:, 3:4], a=stat[:, 3:4], b_=stat[:, 4:5]:
            E.tensor_tensor(o, a, b_, op=OP.subtract), [stat.b], [stat.b])
        vop(lambda E, o=stat[:, 5:6], i=stat[:, 3:4], bb=cst[:, 3:4]:
            E.activation(o, i, AF.Ln, bias=bb), [stat.b, cst.b], [stat.b], eng="S")
        vop(lambda E, o=stat[:, 6:7], i=stat[:, 5:6], sc=cst[:, 2:3]:
            E.activation(o, i, AF.Exp, scale=sc), [stat.b, cst.b], [stat.b], eng="S")
        vop(lambda E, o=dst_bf.ap, i=src.ap, m=stat[:, 1:2], r=stat[:, 6:7]:
            E.tensor_scalar(o, i, m, r, op0=OP.subtract, op1=OP.mult),
            [src.b, stat.b], [dst_bf.b])

    def phase_a(b):
        def load_x_a(tq):
            dma("Y", x_sl[tq % 2].ap, x_in[b, tq * 128:(tq + 1) * 128, :],
                [], [x_sl[tq % 2].b], x_sl[tq % 2].b)
        load_x_a(0)
        for tq in range(c.ntq):
            if tq + 1 < c.ntq:
                load_x_a(tq + 1)
            xs = x_sl[tq % 2]
            h1 = h1_sl[tq % 2]
            layer_norm(xs, h1, tq)
            for k in range(c.nkd):
                vop(lambda E, o=PT.ap, i=h1[:, k * 128:(k + 1) * 128], idd=ident_s.ap:
                    E.transpose(o, i, idd), [h1.b, ident_s.b], [PT.b], eng="T")
                col = tq * 128
                a_i = c.tcd * k + col // c.tc
                a_c = col % c.tc
                vop(lambda E, o=hfm[a_i][:, a_c:a_c + 128], i=PT.ap:
                    E.copy(o, i), [PT.b], [hfm[a_i].b], eng="S")
            yield
        nm = 2 * c.ndch
        ntqh = c.L // c.NT
        wt_ring = [wtile[:c.nkd], wtile[c.nkd:]]
        for m in range(nm + 1):
            if m < nm:
                rr = wt_ring[m % 2]
                for k in range(c.nkd):
                    dma("Y", rr[k][:, 0:128],
                        w_inT_bf[k * 128:(k + 1) * 128, m * 128:(m + 1) * 128],
                        [winbf_b], [rr[k].b], rr[k].b)
            if m == 0:
                continue
            mm = m - 1
            rr = wt_ring[mm % 2]
            for half in range(ntqh // 2):
                for k in range(c.nkd):
                    for thh in range(2):
                        th = 2 * half + thh
                        col = th * c.NT
                        a_i = c.tcd * k + col // c.tc
                        a_c = col % c.tc
                        pz = [P[1], P[3]][thh]
                        vop(lambda E, o=pz.ap, w=rr[k][:, 0:128],
                            r_=hfm[a_i][:, a_c:a_c + c.NT], s=(k == 0), p=(k == c.nkd - 1):
                            E.matmul(o, w, r_, start=s, stop=p),
                            [rr[k].b, hfm[a_i].b], [], eng="T", accum=[pz.b])
                for thh in range(2):
                    th = 2 * half + thh
                    pz = [P[1], P[3]][thh]
                    ev = ev_bf[thh]
                    if mm < c.ndch:
                        vop(lambda E, o=ev.ap, i=pz.ap: E.copy(o, i),
                            [pz.b], [ev.b], eng="S")
                        dma("Y", xp_scr[b, mm * 128:(mm + 1) * 128, th * c.NT:(th + 1) * c.NT],
                            ev.ap, [ev.b], [xp_b[b][mm]], ev.b)
                    else:
                        vop(lambda E, o=ev.ap, i=pz.ap:
                            E.activation(o, i, AF.Silu), [pz.b], [ev.b], eng="S")
                        dma("Y", sz_scr[b, (mm - c.ndch) * 128:(mm - c.ndch + 1) * 128,
                                        th * c.NT:(th + 1) * c.NT],
                            ev.ap, [ev.b], [sz_b[b][mm - c.ndch]], ev.b)
            yield

    def phase_m(b, d, filler=None):
        rev = (d == 1)
        tcs = list(range(c.tcd - 1, -1, -1)) if rev else list(range(c.tcd))
        for tci, tcn in enumerate(tcs):
            t0 = tcn * c.tc
            def load_xp(k):
                xs = xp_sl[k % 5]
                if not rev:
                    lo = t0 - 3
                    if lo < 0:
                        vop(lambda E, o=xs[:, 0:3]: E.memset(o, 0.0), [], [xs.b])
                        dma("Y", xs[:, 3:3 + c.tc], xp_scr[b, k * 128:(k + 1) * 128, t0:t0 + c.tc],
                            [xp_b[b][k]], [xs.b], xs.b)
                    else:
                        dma("Y", xs[:, 0:3 + c.tc], xp_scr[b, k * 128:(k + 1) * 128, lo:t0 + c.tc],
                            [xp_b[b][k]], [xs.b], xs.b)
                else:
                    hi = t0 + c.tc + 3
                    if hi > c.L:
                        dma("Y", xs[:, 0:c.tc], xp_scr[b, k * 128:(k + 1) * 128, t0:t0 + c.tc],
                            [xp_b[b][k]], [xs.b], xs.b)
                        vop(lambda E, o=xs[:, c.tc:c.tc + 3]: E.memset(o, 0.0), [], [xs.b])
                    else:
                        dma("Y", xs[:, 0:c.tc + 3], xp_scr[b, k * 128:(k + 1) * 128, t0:hi],
                            [xp_b[b][k]], [xs.b], xs.b)

            for kk in range(min(4, c.ndch)):
                load_xp(kk)
            for k in range(c.ndch):
                if k + 4 < c.ndch:
                    load_xp(k + 4)
                xs = xp_sl[k % 5]
                win = lambda j: xs[:, j:j + c.tc]
                tap = (lambda j: j) if not rev else (lambda j: 3 - j)
                ac = ca[k % 2]
                ceng = "V"
                vop(lambda E, o=ac.ap, i=win(0), w=convw_s[d][:, k * 4 + tap(0):k * 4 + tap(0) + 1]:
                    E.tensor_scalar(o, i, w, None, op0=OP.mult),
                    [xs.b, convw_s[d].b], [ac.b], eng=ceng)
                for j in range(1, 4):
                    vop(lambda E, o=ac.ap, i=win(j),
                        w=convw_s[d][:, k * 4 + tap(j):k * 4 + tap(j) + 1], a=ac.ap:
                        E.scalar_tensor_tensor(o, i, w, a, op0=OP.mult, op1=OP.add),
                        [xs.b, convw_s[d].b, ac.b], [ac.b], eng=ceng)
                ub = ubf[k % 2]
                vop(lambda E, o=ub.ap, i=ac.ap, bb=convb_s[d][:, k:k + 1]:
                    E.activation(o, i, AF.Silu, bias=bb),
                    [ac.b, convb_s[d].b], [ub.b], eng="S")
                dma("G", u_scr[tci % 2, k * 128:(k + 1) * 128, :], ub.ap,
                    [ub.b], [u_b[tci % 2][k]], ub.b)
                for nn in range(c.nnt):
                    vop(lambda E, o=P[2 + nn][0:c.xrows, :], w=wx_s[d][k].ap,
                        r_=ub[:, nn * c.NT:(nn + 1) * c.NT], s=(k == 0), p=(k == c.ndch - 1):
                        E.matmul(o, w, r_, start=s, stop=p),
                        [wx_s[d][k].b, ub.b], [], eng="T", accum=[P[2 + nn].b])
            for nn in range(c.nnt):
                vop(lambda E, o=xdbl[0:c.xrows, nn * c.NT:(nn + 1) * c.NT], i=P[2 + nn][0:c.xrows, :]:
                    E.copy(o, i), [P[2 + nn].b], [xdbl.b], eng="S")
            dma("G", xdbl_scr[c.dtr:c.xrows, :], xdbl[c.dtr:c.xrows, :],
                [xdbl.b], [xdbl_scr_b], xdbl.b)

            use_carry = (tci > 0)
            save_carry = (tci < c.tcd - 1)
            for s in range(c.ds):
                dma("G", arn[s].ap,
                    xdbl_scr[c.dtr + s:c.dtr + s + 1, :].partition_broadcast(128).squeeze(1),
                    [xdbl_scr_b], [arn[s].b], arn[s].b)
                dma("G", arn[16 + s].ap,
                    xdbl_scr[c.dtr + c.ds + s:c.dtr + c.ds + s + 1, :].partition_broadcast(128).squeeze(1),
                    [xdbl_scr_b], [arn[16 + s].b], arn[16 + s].b)

            def chain_mm(k):
                if k >= c.ndch:
                    return
                us = u_sl[k % 3]
                dma("Y", us.ap, u_scr[tci % 2, k * 128:(k + 1) * 128, :],
                    [u_b[tci % 2][k]], [us.b], us.b)
                for nn in range(c.nnt):
                    vop(lambda E, o=P[4 + (k % 2)].ap, w=wdt_s[d][:, k * 128:(k + 1) * 128],
                        r_=xdbl[0:c.dtr, nn * c.NT:(nn + 1) * c.NT]:
                        E.matmul(o, w, r_, start=True, stop=True),
                        [wdt_s[d].b, xdbl.b], [], eng="T", accum=[P[4 + (k % 2)].b])

            def chain_ops(k):
                if k >= c.ndch:
                    return []
                us = u_sl[k % 3]
                dt_ = dtb[k % 2]
                sA = chA[k % 2]
                sB = chB[k % 2]
                ops = []
                for nn in range(c.nnt):
                    sl = slice(nn * c.NT, (nn + 1) * c.NT)
                    ops.append(lambda sl=sl: vop(
                        lambda E, o=sB[:, sl], i=P[4 + (k % 2)].ap, bb=bdt_s[d][:, k:k + 1]:
                        E.activation(o, i, AF.Identity, bias=bb),
                        [P[4 + (k % 2)].b, bdt_s[d].b], [sB.b], eng="S"))
                    ops.append(lambda sl=sl: vop(
                        lambda E, o=sA[:, sl], i=sB[:, sl]:
                        E.activation(o, i, AF.Abs), [sB.b], [sA.b], eng="S"))
                    ops.append(lambda sl=sl: vop(
                        lambda E, o=sA[:, sl], i=sA[:, sl], sc=cst[:, 1:2]:
                        E.activation(o, i, AF.Exp, scale=sc),
                        [sA.b, cst.b], [sA.b], eng="S"))
                    ops.append(lambda sl=sl: vop(
                        lambda E, o=sA[:, sl], i=sA[:, sl], bb=cst[:, 0:1]:
                        E.activation(o, i, AF.Ln, bias=bb),
                        [sA.b, cst.b], [sA.b], eng="S"))
                    ops.append(lambda sl=sl: vop(
                        lambda E, o=sB[:, sl], i=sB[:, sl]:
                        E.activation(o, i, AF.Relu), [sB.b], [sB.b], eng="S"))
                    ops.append(lambda sl=sl: vop(
                        lambda E, o=dt_[:, sl], a=sA[:, sl], b_=sB[:, sl]:
                        E.tensor_tensor(o, a, b_, op=OP.add),
                        [sA.b, sB.b], [dt_.b]))
                ops.append(lambda: vop(
                    lambda E, o=dtu[k % 2].ap, a=dt_.ap, b_=us.ap:
                    E.tensor_tensor(o, a, b_, op=OP.mult), [dt_.b, us.b], [dtu[k % 2].b]))
                return ops

            def emit_dA(k, s):
                da = dA_t[s % 3]
                vop(lambda E, o=da.ap, i=dtb[k % 2].ap,
                    sc=negA_s[d][:, k * c.ds + s:k * c.ds + s + 1]:
                    E.activation(o, i, AF.Exp, scale=sc),
                    [dtb[k % 2].b, negA_s[d].b], [da.b], eng="S")

            def emit_dtBu(k, s):
                db = dBu[s % 3]
                vop(lambda E, o=db.ap, a=dtu[k % 2].ap, b_=arn[s].ap:
                    E.tensor_tensor(o, a, b_, op=OP.mult),
                    [dtu[k % 2].b, arn[s].b], [db.b])

            def emit_scan(k, s):
                da = dA_t[s % 3]
                db = dBu[s % 3]
                h = hh[s % 3]
                seng = "V"
                if rev:
                    if use_carry:
                        vop(lambda E, o=h[:, c.tc - 1::-1], a=da[:, c.tc - 1::-1],
                            b_=db[:, c.tc - 1::-1], ini=carry[k][:, s:s + 1]:
                            E.tensor_tensor_scan(o, a, b_, ini, op0=OP.mult, op1=OP.add),
                            [da.b, db.b, carry[k].b], [h.b], eng=seng)
                    else:
                        vop(lambda E, o=h[:, c.tc - 1::-1], a=da[:, c.tc - 1::-1],
                            b_=db[:, c.tc - 1::-1]:
                            E.tensor_tensor_scan(o, a, b_, 0.0, op0=OP.mult, op1=OP.add),
                            [da.b, db.b], [h.b], eng=seng)
                else:
                    if use_carry:
                        vop(lambda E, o=h.ap, a=da.ap, b_=db.ap, ini=carry[k][:, s:s + 1]:
                            E.tensor_tensor_scan(o, a, b_, ini, op0=OP.mult, op1=OP.add),
                            [da.b, db.b, carry[k].b], [h.b], eng=seng)
                    else:
                        vop(lambda E, o=h.ap, a=da.ap, b_=db.ap:
                            E.tensor_tensor_scan(o, a, b_, 0.0, op0=OP.mult, op1=OP.add),
                            [da.b, db.b], [h.b], eng=seng)
                if save_carry:
                    col = 0 if rev else c.tc - 1
                    vop(lambda E, o=carry[k][:, s:s + 1], i=h[:, col:col + 1]:
                        E.copy(o, i), [h.b], [carry[k].b], eng="S")

            def emit_cmult_y(k, s):
                h = hh[s % 3]
                hc = hcc[s % 2]
                vop(lambda E, o=hc.ap, a=h.ap, b_=arn[16 + s].ap:
                    E.tensor_tensor(o, a, b_, op=OP.mult),
                    [h.b, arn[16 + s].b], [hc.b], eng=("G" if s % 2 else "V"))
                py = P[0] if k % 2 == 0 else P[2]
                for nn in range(c.nnt):
                    vop(lambda E, o=py.ap, w=ident_s.ap,
                        r_=hc[:, nn * c.NT:(nn + 1) * c.NT],
                        s_=(s == 0), p=(s == c.ds - 1):
                        E.matmul(o, w, r_, start=s_, stop=p),
                        [ident_s.b, hc.b], [], eng="T", accum=[py.b])

            sz_hold = {}

            def epilogue(k):
                us = u_sl[k % 3]
                t1 = acc_f[k % 2]
                py = P[0] if k % 2 == 0 else P[2]
                for nn in range(c.nnt):
                    vop(lambda E, o=t1[:, nn * c.NT:(nn + 1) * c.NT],
                        i=us[:, nn * c.NT:(nn + 1) * c.NT],
                        dd=Dd_s[d][:, k:k + 1], p_=py.ap:
                        E.scalar_tensor_tensor(o, i, dd, p_, op0=OP.mult, op1=OP.add),
                        [us.b, Dd_s[d].b, py.b], [t1.b])
                vop(lambda E, o=yg[k].ap, a=t1.ap, b_=ss_hold[k].ap:
                    E.tensor_tensor(o, a, b_, op=OP.mult),
                    [t1.b, ss_hold[k].b], [yg[k].b])

            chain_mm(0)
            for f in chain_ops(0):
                f()
            ss_hold = {}
            for k in range(c.ndch):
                if filler is not None:
                    filler()
                ss = sz_sl[k % 2]
                ss_hold[k] = ss
                dma("Y", ss.ap, sz_scr[b, k * 128:(k + 1) * 128, t0:t0 + c.tc],
                    [sz_b[b][k]], [ss.b], ss.b)
                chain_mm(k + 1)
                nxt = chain_ops(k + 1)
                ni = 0
                for i in range(c.ds + 2):
                    if i < c.ds:
                        emit_dA(k, i)
                        emit_dtBu(k, i)
                    if 2 <= i < c.ds + 2 and ni < len(nxt):
                        want = min(len(nxt), (i - 1) * len(nxt) // 6)
                        while ni <= want and ni < len(nxt):
                            nxt[ni]()
                            ni += 1
                    if 1 <= i < c.ds + 1:
                        emit_scan(k, i - 1)
                    if i >= 2:
                        emit_cmult_y(k, i - 2)
                    if i == 2 and k >= 1:
                        epilogue(k - 1)
                while ni < len(nxt):
                    nxt[ni]()
                    ni += 1
            epilogue(c.ndch - 1)

            tqs = list(range(c.ntq_tc))
            blocks = [tqs[i:i + 3] for i in range(0, len(tqs), 3)]
            for blk_tqs in blocks:
                def load_wout(k):
                    ws = wout_sl[k % 4]
                    dma("Y", ws.ap, w_outT_bf[k * 128:(k + 1) * 128, :],
                        [woutbf_b], [ws.b], ws.b)
                for kk in range(min(3, c.ndch)):
                    load_wout(kk)
                for k in range(c.ndch):
                    if k + 3 < c.ndch:
                        load_wout(k + 3)
                    ws = wout_sl[k % 4]
                    for ti, tq in enumerate(blk_tqs):
                        for hj in range(c.dm // c.NT):
                            vop(lambda E, o=P[2 * ti + hj].ap,
                                w=yg[k][:, tq * 128:(tq + 1) * 128],
                                r_=ws[:, hj * c.NT:(hj + 1) * c.NT],
                                s_=(k == 0), p=(k == c.ndch - 1):
                                E.matmul(o, w, r_, start=s_, stop=p),
                                [yg[k].b, ws.b], [], eng="T", accum=[P[2 * ti + hj].b])
                for ti, tq in enumerate(blk_tqs):
                    row = t0 + tq * 128
                    gq = row // 128
                    ps = pstg[0]
                    if not rev:
                        for hj in range(c.dm // c.NT):
                            vop(lambda E, o=ps[:, hj * c.NT:(hj + 1) * c.NT],
                                i=P[2 * ti + hj].ap:
                                E.copy(o, i), [P[2 * ti + hj].b], [ps.b], eng="S")
                    else:
                        pf = pfld[ti % 2]
                        dma("Y", pf.ap, p_scr[b, row:row + 128, :], [p_b[b][gq]], [pf.b], pf.b)
                        for hj in range(c.dm // c.NT):
                            vop(lambda E, o=ps[:, hj * c.NT:(hj + 1) * c.NT],
                                i=pf[:, hj * c.NT:(hj + 1) * c.NT], p_=P[2 * ti + hj].ap:
                                E.scalar_tensor_tensor(o, i, 1.0, p_, op0=OP.mult, op1=OP.add),
                                [pf.b, P[2 * ti + hj].b], [ps.b])
                    dma("Y", p_scr[b, row:row + 128, :], ps.ap,
                        [ps.b], [p_b[b][gq]], ps.b)

    def phase_f(b):
        def load_xp_f(tq):
            dma("Y", x_sl[tq % 2].ap, x_in[b, tq * 128:(tq + 1) * 128, :],
                [], [x_sl[tq % 2].b], x_sl[tq % 2].b)
            dma("Y", pfld[tq % 2].ap, p_scr[b, tq * 128:(tq + 1) * 128, :],
                [p_b[b][tq]], [pfld[tq % 2].b], pfld[tq % 2].b)
        load_xp_f(0)
        for tq in range(c.ntq):
            if tq + 1 < c.ntq:
                load_xp_f(tq + 1)
            xs = x_sl[tq % 2]
            pf = pfld[tq % 2]
            o1 = out1t[tq % 2]
            h2 = h2_sl[tq % 2]
            vop(lambda E, o=o1.ap, a=xs.ap, b_=pf.ap:
                E.tensor_tensor(o, a, b_, op=OP.add), [xs.b, pf.b], [o1.b])
            layer_norm(o1, h2, tq)
            for k in range(c.nkd):
                vop(lambda E, o=PT.ap, i=h2[:, k * 128:(k + 1) * 128], idd=ident_s.ap:
                    E.transpose(o, i, idd), [h2.b, ident_s.b], [PT.b], eng="T")
                vop(lambda E, o=h2T[k].ap, i=PT.ap: E.copy(o, i),
                    [PT.b], [h2T[k].b], eng="S")
            pm = [P[0], P[1]] if tq % 2 == 0 else [P[2], P[3]]
            pax = PAX
            nh = c.dm // c.NT
            for k in range(c.nkd):
                bw = bw_sl[k % 2]
                dma("Y", bw.ap, basewT_bf[k * 128:(k + 1) * 128, :],
                    [basebf_b], [bw.b], bw.b)
                for hj in range(nh):
                    vop(lambda E, o=pm[hj].ap, w=h2T[k].ap,
                        r_=bw[:, hj * c.NT:(hj + 1) * c.NT], s_=(k == 0):
                        E.matmul(o, w, r_, start=s_, stop=False),
                        [h2T[k].b, bw.b], [], eng="T", accum=[pm[hj].b])
                vop(lambda E, o=pax.ap, w=h2T[k].ap, r_=laA_s[k].ap, s_=(k == 0),
                    p=(k == c.nkd - 1):
                    E.matmul(o, w, r_, start=s_, stop=p),
                    [h2T[k].b, laA_s[k].b], [], eng="T", accum=[pax.b])
            for hj in range(nh):
                vop(lambda E, o=pm[hj].ap, w=ones_s.ap,
                    r_=bb_s[0:1, hj * c.NT:(hj + 1) * c.NT]:
                    E.matmul(o, w, r_, start=False, stop=False),
                    [ones_s.b, bb_s.b], [], eng="T", accum=[pm[hj].b])
            vop(lambda E, o=l8.ap, i=pax[:, 128:136], r_=rb_s.ap:
                E.tensor_tensor(o, i, r_, op=OP.add), [pax.b, rb_s.b], [l8.b])
            vop(lambda E, o=st2[:, 0:1], i=l8.ap: E.reduce_max(o, i, axis=AX.X),
                [l8.b], [st2.b])
            vop(lambda E, o=e8.ap, i=l8.ap, m=st2[:, 0:1]:
                E.tensor_scalar(o, i, m, None, op0=OP.is_ge), [l8.b, st2.b], [e8.b])
            vop(lambda E, o=st2[:, 1:2], i=e8.ap: E.reduce_sum(o, i, axis=AX.X),
                [e8.b], [st2.b])
            vop(lambda E, o=e8.ap, i=e8.ap, l=l8.ap:
                E.scalar_tensor_tensor(o, i, -1e30, l, op0=OP.mult, op1=OP.add),
                [e8.b, l8.b], [e8.b])
            vop(lambda E, o=st2[:, 2:3], i=e8.ap: E.reduce_max(o, i, axis=AX.X),
                [e8.b], [st2.b])
            vop(lambda E, o=st2[:, 3:4], i=st2[:, 1:2]:
                E.tensor_scalar(o, i, 2.0, None, op0=OP.is_ge), [st2.b], [st2.b])
            vop(lambda E, o=st2[:, 4:5], a=st2[:, 0:1], b_=st2[:, 2:3]:
                E.tensor_tensor(o, a, b_, op=OP.subtract), [st2.b], [st2.b])
            vop(lambda E, o=st2[:, 4:5], a=st2[:, 4:5], b_=st2[:, 3:4]:
                E.tensor_tensor(o, a, b_, op=OP.mult), [st2.b], [st2.b])
            vop(lambda E, o=st2[:, 4:5], a=st2[:, 4:5], b_=st2[:, 2:3]:
                E.tensor_tensor(o, a, b_, op=OP.add), [st2.b], [st2.b])
            vop(lambda E, o=e8.ap, i=l8.ap, kk=st2[:, 4:5]:
                E.tensor_scalar(o, i, kk, None, op0=OP.is_lt), [l8.b, st2.b], [e8.b])
            vop(lambda E, o=l8.ap, i=e8.ap, l=l8.ap:
                E.scalar_tensor_tensor(o, i, -1e9, l, op0=OP.mult, op1=OP.add),
                [e8.b, l8.b], [l8.b])
            vop(lambda E, o=l8.ap, i=l8.ap, m=st2[:, 0:1]:
                E.tensor_scalar(o, i, m, None, op0=OP.subtract), [l8.b, st2.b], [l8.b])
            vop(lambda E, o=e8.ap, i=l8.ap: E.activation(o, i, AF.Exp),
                [l8.b], [e8.b], eng="S")
            vop(lambda E, o=st2[:, 5:6], i=e8.ap: E.reduce_sum(o, i, axis=AX.X),
                [e8.b], [st2.b])
            vop(lambda E, o=st2[:, 6:7], i=st2[:, 5:6]: E.reciprocal(o, i),
                [st2.b], [st2.b])
            vop(lambda E, o=w2b.ap, i=e8.ap, r_=st2[:, 6:7]:
                E.tensor_scalar(o, i, r_, 2.0, op0=OP.mult, op1=OP.mult),
                [e8.b, st2.b], [w2b.b])
            vop(lambda E, o=axb.ap, i=pax[:, 0:128]: E.tensor_copy(o, i),
                [pax.b], [axb.b])
            vop(lambda E, o=axw.ap.rearrange("p (e r) -> p e r", r=c.ds),
                a=axb.ap.rearrange("p (e r) -> p e r", r=c.ds),
                b_=w2b.ap.unsqueeze(2).broadcast_to((128, 8, c.ds)):
                E.tensor_tensor(o, a, b_, op=OP.mult),
                [axb.b, w2b.b], [axw.b])
            vop(lambda E, o=PT.ap, i=axw.ap, idd=ident_s.ap:
                E.transpose(o, i, idd), [axw.b, ident_s.b], [PT.b], eng="T")
            vop(lambda E, o=axwT.ap, i=PT.ap: E.copy(o, i), [PT.b], [axwT.b], eng="S")
            for hj in range(nh):
                vop(lambda E, o=pm[hj].ap, w=axwT.ap,
                    r_=laB_s[:, hj * c.NT:(hj + 1) * c.NT]:
                    E.matmul(o, w, r_, start=False, stop=True),
                    [axwT.b, laB_s.b], [], eng="T", accum=[pm[hj].b])
            for hj in range(nh):
                os = outst[hj % 2]
                vop(lambda E, o=os.ap, i=o1[:, hj * c.NT:(hj + 1) * c.NT], p_=pm[hj].ap:
                    E.scalar_tensor_tensor(o, i, 1.0, p_, op0=OP.mult, op1=OP.add),
                    [o1.b, pm[hj].b], [os.b])
                dma("Y", out_d[b, tq * 128:(tq + 1) * 128, hj * c.NT:(hj + 1) * c.NT],
                    os.ap, [os.b], [], os.b)
            yield

    a_gens = [phase_a(b) for b in range(c.nb)]
    f_gens = [phase_f(b) for b in range(c.nb)]
    for _ in a_gens[0]:
        pass
    for b in range(c.nb):
        nxt = a_gens[b + 1] if b + 1 < c.nb else None
        fprev = f_gens[b - 1] if b >= 1 else None

        def filler(n=1, ga=nxt, gf=fprev):
            for g in (gf, ga):
                if g is not None and next(g, "DONE") != "DONE":
                    return
        phase_m(b, 0, filler)
        phase_m(b, 1, filler)
        if nxt is not None:
            for _ in nxt:
                pass
        if fprev is not None:
            for _ in fprev:
                pass
    for _ in f_gens[c.nb - 1]:
        pass

    pl.barrier("Y", [outst[0].b, outst[1].b])
    pl.barrier("G", [outst[0].b, outst[1].b])

    pl.finalize()
    eng_sems = {}
    dma_sems = {}
    with stk:
        for e in ("T", "V", "S", "G"):
            eng_sems[e] = stk.enter_context(nc.semaphore(f"es_{e}"))
        for key in pl.dma_count:
            dma_sems[key] = stk.enter_context(nc.semaphore(f"ds_{key[1]}"))

        def emit_engine(E, e):
            for o in pl.eng_ops[e]:
                for (sk, val) in o.waits:
                    h = eng_sems[sk] if sk in pl.eng_ops else dma_sems[sk]
                    E.wait_ge(h, val)
                if o.fn is None:
                    continue
                inst = o.fn(E)
                if o.dma_key is not None:
                    inst.then_inc(dma_sems[o.dma_key], 16)
                elif o.inc:
                    inst.then_inc(eng_sems[e], 1)

        with nc.Block() as blk:
            @blk.tensor
            def _(E):
                emit_engine(E, "T")

            @blk.vector
            def _(E):
                emit_engine(E, "V")

            @blk.scalar
            def _(E):
                emit_engine(E, "S")

            @blk.gpsimd
            def _(E):
                emit_engine(E, "G")

            @blk.sync
            def _(E):
                emit_engine(E, "Y")

    return nc




def prep_inputs(cfg, full):
    c = cfg
    x = np.ascontiguousarray(full["L0_attn_hidden_states"])
    f32 = np.float32

    def t(a):
        return np.ascontiguousarray(np.asarray(a).T)

    def chunk_cols(a, width=128):
        a = np.asarray(a)
        if a.ndim == 1:
            return np.ascontiguousarray(a.reshape(-1, width).T)
        n = a.shape[0] // width
        return np.ascontiguousarray(
            a.reshape(n, width, a.shape[1]).transpose(1, 0, 2).reshape(width, -1))

    shared = {
        "w_inT": t(full["L0_attn_W_in"]).astype(f32),
        "w_outT": t(full["L0_attn_W_out"]).astype(f32),
        "basewT": t(full["L0_ffn_base_w"]).astype(f32),
        "baseb": np.asarray(full["L0_ffn_base_b"], f32).reshape(1, -1),
        "laA": np.ascontiguousarray(
            np.asarray(full["L0_ffn_lora_A"], f32).transpose(1, 0, 2).reshape(c.dm, -1)),
        "laB": np.ascontiguousarray(
            np.asarray(full["L0_ffn_lora_B"], f32).reshape(-1, c.dm)),
        "rw": t(full["L0_ffn_router_w"]).astype(f32),
        "rb": np.asarray(full["L0_ffn_router_b"], f32).reshape(1, -1),
        "ident": np.eye(128).astype(BFNP),
        "ones1": np.ones((1, 128), f32),
    }
    for d, sfx in enumerate(("f", "r")):
        shared[f"wxT_{d}"] = t(full[f"L0_attn_W_x_{sfx}"]).astype(f32)
        shared[f"wdtT_{d}"] = t(full[f"L0_attn_W_dt_{sfx}"]).astype(f32)
        shared[f"convw_{d}"] = np.ascontiguousarray(
            np.asarray(full[f"L0_attn_conv_w_{sfx}"], f32)
            .reshape(c.ndch, 128, 4).transpose(1, 0, 2).reshape(128, -1))
        shared[f"convb_{d}"] = chunk_cols(np.asarray(full[f"L0_attn_conv_b_{sfx}"], f32))
        shared[f"bdt_{d}"] = chunk_cols(np.asarray(full[f"L0_attn_b_dt_{sfx}"], f32))
        shared[f"Dvec_{d}"] = chunk_cols(np.asarray(full[f"L0_attn_D_{sfx}"], f32))
        shared[f"alog_{d}"] = np.ascontiguousarray(
            np.asarray(full[f"L0_attn_A_log_{sfx}"], f32)
            .reshape(c.ndch, 128, c.ds).transpose(1, 0, 2).reshape(128, -1))

    in_maps = []
    for i in range(N_CORES):
        m = dict(shared)
        m["x"] = np.ascontiguousarray(x[i * c.nb:(i + 1) * c.nb], f32)
        in_maps.append(m)
    return in_maps


def kernel(**inputs):
    cfg = CFG(dm=1024, L=2048, nb=32 // N_CORES)
    nc = build(cfg)
    in_maps = prep_inputs(cfg, inputs)
    res = run_bass_kernel_spmd(nc, in_maps, core_ids=list(range(N_CORES)))
    out = np.concatenate([np.asarray(r["out"], np.float32) for r in res.results], axis=0)
    return np.ascontiguousarray(out)


# version 25
# speedup vs baseline: 1.1048x; 1.1048x over previous
import numpy as np
import ml_dtypes
from contextlib import ExitStack

import concourse.bass as bass
import concourse.mybir as mybir
from concourse.alu_op_type import AluOpType as OP
from concourse.bass_utils import run_bass_kernel_spmd

F32 = mybir.dt.float32
BF16 = mybir.dt.bfloat16
AF = mybir.ActivationFunctionType
AX = mybir.AxisListType
BFNP = ml_dtypes.bfloat16

N_CORES = 8


class CFG:
    def __init__(self, dm=1024, L=2048, nb=4):
        self.dm = dm
        self.L = L
        self.nb = nb
        self.dI = 2 * dm
        self.ds = 16
        self.dtr = dm // 16
        self.xrows = self.dtr + 2 * self.ds
        self.ndch = self.dI // 128
        self.nkd = dm // 128
        self.tc = L // 4
        self.NT = min(512, self.tc)
        self.nnt = self.tc // self.NT
        self.ntq = L // 128
        self.ntq_tc = self.tc // 128
        self.tcd = L // self.tc



ENGINES = ("T", "V", "S", "G", "Y")


class Buf:
    __slots__ = ("name", "w", "readers", "dma_key")

    def __init__(self, name):
        self.name = name
        self.w = None
        self.readers = []
        self.dma_key = {}


class POp:
    __slots__ = ("eng", "fn", "waits", "tok", "dma_key", "inc", "pre_drain")

    def __init__(self, eng, fn, waits, tok, dma_key):
        self.eng = eng
        self.fn = fn
        self.waits = waits
        self.tok = tok
        self.dma_key = dma_key
        self.inc = False
        self.pre_drain = False


class Plan:
    def __init__(self):
        self.eng_ops = {e: [] for e in ENGINES}
        self.eng_n = {e: 0 for e in ENGINES}
        self.waited = {e: {} for e in ENGINES}
        self.dma_count = {}
        self.awaited = set()
        self.n_dma_keys = 0
        self.drained = {e: 0 for e in ENGINES}

    def buf(self, name):
        return Buf(name)

    def _dma_key_for(self, b, qcls):
        if qcls not in b.dma_key:
            b.dma_key[qcls] = ("D", self.n_dma_keys, b.name + "_" + qcls)
            self.n_dma_keys += 1
            self.dma_count[b.dma_key[qcls]] = 0
        return b.dma_key[qcls]

    def _waits(self, eng, reads, writes, accum=()):
        deps = []
        for b in reads:
            if b.w is not None:
                deps.append(b.w)
        for b in writes:
            if b.w is not None:
                deps.append(b.w)
            deps.extend(b.readers)
        for b in accum:
            if b.w is not None and b.w[0] != "T":
                deps.append(b.w)
            deps.extend(b.readers)
        out = []
        wmap = self.waited[eng]
        for (sk, val) in deps:
            cur = wmap.get(sk, 0)
            if val <= cur:
                continue
            wmap[sk] = val
            out.append((sk, val))
            if sk in self.eng_ops:
                self.awaited.add((sk, val))
        return out

    def barrier(self, eng, bufs):
        deps = []
        for b in bufs:
            if b.w is not None:
                deps.append(b.w)
            deps.extend(b.readers)
        waits = []
        wmap = self.waited[eng]
        for (sk, val) in deps:
            cur = wmap.get(sk, 0)
            if val <= cur:
                continue
            wmap[sk] = val
            waits.append((sk, val))
            if sk in self.eng_ops:
                self.awaited.add((sk, val))
        self.eng_n[eng] += 1
        o = POp(eng, None, waits, (eng, self.eng_n[eng]), None)
        self.eng_ops[eng].append(o)
        return o

    def op(self, eng, fn, reads=(), writes=(), accum=()):
        waits = self._waits(eng, reads, writes, accum)
        self.eng_n[eng] += 1
        tok = (eng, self.eng_n[eng])
        o = POp(eng, fn, waits, tok, None)
        self.eng_ops[eng].append(o)
        for b in writes:
            b.w = tok
            b.readers = []
        for b in accum:
            b.w = tok
            b.readers = []
        for b in reads:
            b.readers.append(tok)
        return o

    def dma(self, qeng, fn, reads=(), writes=(), sem_buf=None):
        waits = self._waits(qeng, reads, writes)
        key = self._dma_key_for(sem_buf, "G" if qeng == "G" else "H")
        self.dma_count[key] += 16
        tok = (key, self.dma_count[key])
        o = POp(qeng, fn, waits, tok, key)
        self.eng_ops[qeng].append(o)
        for b in writes:
            b.w = tok
            b.readers = []
        for b in reads:
            b.readers.append(tok)
        return o

    def finalize(self):
        rank = {}
        for e in ENGINES:
            ords = sorted(v for (ee, v) in self.awaited if ee == e)
            r = {}
            for i, v in enumerate(ords):
                r[v] = i + 1
            rank[e] = r
            for o in self.eng_ops[e]:
                if (e, o.tok[1]) in self.awaited:
                    o.inc = True
        for e in ENGINES:
            for o in self.eng_ops[e]:
                o.waits = [
                    (sk, rank[sk][val] if sk in self.eng_ops else val)
                    for (sk, val) in o.waits
                ]
        return rank




def build(cfg):
    c = cfg
    nc = bass.Bass("TRN2", target_bir_lowering=False)
    pl = Plan()

    DT = nc.dram_tensor
    x_in = DT("x", [c.nb, c.L, c.dm], F32, kind="ExternalInput").ap()
    out_d = DT("out", [c.nb, c.L, c.dm], F32, kind="ExternalOutput").ap()
    w_inT = DT("w_inT", [c.dm, 2 * c.dI], F32, kind="ExternalInput").ap()
    w_outT = DT("w_outT", [c.dI, c.dm], F32, kind="ExternalInput").ap()
    wxT = [DT(f"wxT_{d}", [c.dI, c.xrows], F32, kind="ExternalInput").ap() for d in range(2)]
    wdtT = [DT(f"wdtT_{d}", [c.dtr, c.dI], F32, kind="ExternalInput").ap() for d in range(2)]
    convw_d = [DT(f"convw_{d}", [128, 4 * c.ndch], F32, kind="ExternalInput").ap() for d in range(2)]
    convb_d = [DT(f"convb_{d}", [128, c.ndch], F32, kind="ExternalInput").ap() for d in range(2)]
    bdt_d = [DT(f"bdt_{d}", [128, c.ndch], F32, kind="ExternalInput").ap() for d in range(2)]
    Dd_d = [DT(f"Dvec_{d}", [128, c.ndch], F32, kind="ExternalInput").ap() for d in range(2)]
    alog_d = [DT(f"alog_{d}", [128, c.ndch * c.ds], F32, kind="ExternalInput").ap() for d in range(2)]
    basewT_d = DT("basewT", [c.dm, c.dm], F32, kind="ExternalInput").ap()
    baseb_d = DT("baseb", [1, c.dm], F32, kind="ExternalInput").ap()
    laA_d = DT("laA", [c.dm, 128], F32, kind="ExternalInput").ap()
    laB_d = DT("laB", [128, c.dm], F32, kind="ExternalInput").ap()
    rw_d = DT("rw", [c.dm, 8], F32, kind="ExternalInput").ap()
    rb_d = DT("rb", [1, 8], F32, kind="ExternalInput").ap()
    ident_d = DT("ident", [128, 128], BF16, kind="ExternalInput").ap()
    ones1_d = DT("ones1", [1, 128], F32, kind="ExternalInput").ap()

    xp_scr = DT("xp_scr", [c.nb, c.dI, c.L], BF16, kind="Internal").ap()
    sz_scr = DT("sz_scr", [c.nb, c.dI, c.L], BF16, kind="Internal").ap()
    u_scr = DT("u_scr", [2, c.dI, c.tc], BF16, kind="Internal").ap()
    p_scr = DT("p_scr", [c.nb, c.L, c.dm], F32, kind="Internal").ap()
    w_inT_bf = DT("w_inT_bf", [c.dm, 2 * c.dI], BF16, kind="Internal").ap()
    xdbl_scr = DT("xdbl_scr", [c.xrows, c.tc], BF16, kind="Internal").ap()
    w_outT_bf = DT("w_outT_bf", [c.dI, c.dm], BF16, kind="Internal").ap()
    basewT_bf = DT("basewT_bf", [c.dm, c.dm], BF16, kind="Internal").ap()

    xp_b = [[pl.buf(f"xp{b}_{k}") for k in range(c.ndch)] for b in range(c.nb)]
    sz_b = [[pl.buf(f"sz{b}_{k}") for k in range(c.ndch)] for b in range(c.nb)]
    u_b = [[pl.buf(f"u{p}_{k}") for k in range(c.ndch)] for p in range(2)]
    p_b = [[pl.buf(f"p{b}_{q}") for q in range(c.ntq)] for b in range(c.nb)]
    winbf_b = pl.buf("winbf")
    xdbl_scr_b = pl.buf("xdbl_scr")
    woutbf_b = pl.buf("woutbf")
    basebf_b = pl.buf("basebf")

    stk = ExitStack()
    SB = lambda name, shp, dt: stk.enter_context(nc.sbuf_tensor(name, shp, dt))
    PS = lambda name, shp, dt: stk.enter_context(nc.psum_tensor(name, shp, dt))

    class Tile:
        def __init__(self, name, shp, dt, psum=False):
            name = ("ps_" if psum else "sb_") + name
            self.t = PS(name, shp, dt) if psum else SB(name, shp, dt)
            self.b = pl.buf(name)

        def __getitem__(self, sl):
            return self.t[sl]

        @property
        def ap(self):
            return self.t[:]

    arn = [Tile(f"arn{i}", [128, c.tc], BF16) for i in range(32)]
    hfm = [Tile(f"hfm{i}", [128, c.tc], BF16) for i in range(4 * c.nkd)]
    wtile = [Tile(f"wt{i}", [128, 128], BF16) for i in range(2 * c.nkd)]
    yg = [Tile(f"yg{i}", [128, c.tc], BF16) for i in range(c.ndch)]
    wx_s = [[Tile(f"wx{d}_{k}", [128, c.xrows], BF16) for k in range(c.ndch)] for d in range(2)]
    wdt_s = [Tile(f"wdt{d}", [c.dtr, c.dI], BF16) for d in range(2)]
    convw_s = [Tile(f"convw{d}", [128, 4 * c.ndch], F32) for d in range(2)]
    convb_s = [Tile(f"convb{d}", [128, c.ndch], F32) for d in range(2)]
    bdt_s = [Tile(f"bdt{d}", [128, c.ndch], F32) for d in range(2)]
    Dd_s = [Tile(f"Dd{d}", [128, c.ndch], F32) for d in range(2)]
    negA_s = [Tile(f"negA{d}", [128, c.ndch * c.ds], F32) for d in range(2)]
    laA_s = [Tile(f"laA{k}", [128, 136], BF16) for k in range(c.nkd)]
    laB_s = Tile("laB", [128, c.dm], BF16)
    rb_s = Tile("rb", [128, 8], F32)
    bb_s = Tile("bb", [1, c.dm], F32)
    ones_s = Tile("ones", [1, 128], F32)
    ident_s = Tile("identt", [128, 128], BF16)

    x_sl = [Tile(f"x_sl{i}", [128, c.dm], F32) for i in range(2)]
    h1_sl = [Tile(f"h1_sl{i}", [128, c.dm], BF16) for i in range(2)]
    ev_bf = [Tile(f"ev_bf{i}", [128, c.NT], BF16) for i in range(2)]
    xp_sl = [Tile(f"xp_sl{i}", [128, c.tc + 6], BF16) for i in range(5)]
    acc_f = [Tile(f"acc_f{i}", [128, c.tc], F32) for i in range(2)]
    ca = [Tile(f"ca{i}", [128, c.tc], BF16) for i in range(2)]
    chA = [Tile(f"chA{i}", [128, c.tc], BF16) for i in range(2)]
    chB = [Tile(f"chB{i}", [128, c.tc], BF16) for i in range(2)]
    ubf = [Tile(f"ubf{i}", [128, c.tc], BF16) for i in range(2)]
    u_sl = [Tile(f"u_sl{i}", [128, c.tc], BF16) for i in range(3)]
    sz_sl = [Tile(f"sz_sl{i}", [128, c.tc], BF16) for i in range(2)]
    xdbl = Tile("xdbl", [max(c.xrows, 1), c.tc], BF16)
    dtb = [Tile(f"dtb{i}", [128, c.tc], BF16) for i in range(2)]
    dtu = [Tile(f"dtu{i}", [128, c.tc], BF16) for i in range(2)]
    dA_t = [Tile(f"dA{i}", [128, c.tc], BF16) for i in range(3)]
    dBu = [Tile(f"dBu{i}", [128, c.tc], BF16) for i in range(3)]
    hh = [Tile(f"hh{i}", [128, c.tc], BF16) for i in range(3)]
    hcc = [Tile(f"hcc{i}", [128, c.tc], BF16) for i in range(2)]
    carry = [Tile(f"carry{k}", [128, c.ds], BF16) for k in range(c.ndch)]
    wout_sl = [Tile(f"wout_sl{i}", [128, c.dm], BF16) for i in range(4)]
    pfld = [Tile(f"pfld{i}", [128, c.dm], F32) for i in range(2)]
    pstg = [Tile(f"pstg{i}", [128, c.dm], F32) for i in range(1)]
    out1t = [Tile(f"out1t{i}", [128, c.dm], F32) for i in range(2)]
    h2_sl = [Tile(f"h2_sl{i}", [128, c.dm], BF16) for i in range(2)]
    h2T = [Tile(f"h2T{k}", [128, 128], BF16) for k in range(c.nkd)]
    bw_sl = [Tile(f"bw_sl{i}", [128, c.dm], BF16) for i in range(2)]
    trash = Tile("trash", [128, c.dm], BF16)
    stat = Tile("stat", [128, 8], F32)
    cst = Tile("cst", [128, 4], F32)
    st2 = Tile("st2", [128, 8], F32)
    l8 = Tile("l8", [128, 8], F32)
    e8 = Tile("e8", [128, 8], F32)
    w2b = Tile("w2b", [128, 8], BF16)
    axb = Tile("axb", [128, 128], BF16)
    axw = Tile("axw", [128, 128], BF16)
    axwT = Tile("axwT", [128, 128], BF16)
    outst = [Tile(f"outst{i}", [128, c.NT], F32) for i in range(2)]

    P = [Tile(f"P{i}", [128, c.NT], F32, psum=True) for i in range(6)]
    PT = Tile("PTb", [128, 128], BF16, psum=True)
    PAX = Tile("PAXb", [128, 136], F32, psum=True)

    def dma(q, dst_ap, src_ap, reads, writes, sem_buf):
        pl.dma(q, lambda E, d=dst_ap, s=src_ap: E.dma_start(out=d, in_=s),
               reads=reads, writes=writes, sem_buf=sem_buf)

    def vop(fn, reads=(), writes=(), eng="V", accum=()):
        pl.op(eng, fn, reads=reads, writes=writes, accum=accum)

    for ci, cv in enumerate((1.0, -1.0, -0.5, 1e-5)):
        vop(lambda E, o=cst[:, ci:ci + 1], v=cv: E.memset(o, v), [], [cst.b])

    def convert_weight(src_ap, dst_ap, rows, cols, dst_buf):
        nrch = rows // 128
        ncch = (cols + c.tc - 1) // c.tc
        for r in range(nrch):
            for cc in range(ncch):
                w = min(c.tc, cols - cc * c.tc)
                fst = pfld[(r * ncch + cc) % 2]
                bst = u_sl[(r * ncch + cc) % 2]
                for off in range(0, w, c.dm):
                    ww = min(c.dm, w - off)
                    dma("Y", fst[:, 0:ww],
                        src_ap[r * 128:(r + 1) * 128, cc * c.tc + off:cc * c.tc + off + ww],
                        [], [fst.b], fst.b)
                    vop(lambda E, a=bst[:, off:off + ww], b_=fst[:, 0:ww]:
                        E.tensor_copy(a, b_), [fst.b], [bst.b])
                dma("Y", dst_ap[r * 128:(r + 1) * 128, cc * c.tc:cc * c.tc + w],
                    bst[:, 0:w], [bst.b], [dst_buf], bst.b)

    convert_weight(w_inT, w_inT_bf, c.dm, 2 * c.dI, winbf_b)
    convert_weight(w_outT, w_outT_bf, c.dI, c.dm, woutbf_b)
    convert_weight(basewT_d, basewT_bf, c.dm, c.dm, basebf_b)

    def load_small_f32(tile, src_ap):
        dma("Y", tile.ap, src_ap, [], [tile.b], tile.b)

    def load_small_bf16(tile, src_ap, rows, cols):
        for off in range(0, cols, c.dm):
            w = min(c.dm, cols - off)
            fst = pfld[(off // c.dm) % 2]
            dma("Y", fst[0:rows, 0:w], src_ap[:, off:off + w], [], [fst.b], fst.b)
            vop(lambda E, a=tile[0:rows, off:off + w], b_=fst[0:rows, 0:w]:
                E.tensor_copy(a, b_), [fst.b], [tile.b])

    for d in range(2):
        for k in range(c.ndch):
            load_small_bf16(wx_s[d][k], wxT[d][k * 128:(k + 1) * 128, :], 128, c.xrows)
        load_small_bf16(wdt_s[d], wdtT[d], c.dtr, c.dI)
        load_small_f32(convw_s[d], convw_d[d])
        load_small_f32(convb_s[d], convb_d[d])
        load_small_f32(bdt_s[d], bdt_d[d])
        load_small_f32(Dd_s[d], Dd_d[d])
        fst = pfld[1]
        dma("Y", fst[:, 0:c.ndch * c.ds], alog_d[d], [], [fst.b], fst.b)
        vop(lambda E, o=negA_s[d].ap, i=fst[:, 0:c.ndch * c.ds]:
            E.activation(o, i, AF.Exp), [fst.b], [negA_s[d].b], eng="S")
        vop(lambda E, o=negA_s[d].ap: E.tensor_scalar(o, o, -1.0, None, op0=OP.mult),
            [negA_s[d].b], [negA_s[d].b])
    for k in range(c.nkd):
        load_small_bf16(laA_s[k], laA_d[k * 128:(k + 1) * 128, :], 128, 128)
        fst2 = pfld[0]
        dma("Y", fst2[0:128, 0:8], rw_d[k * 128:(k + 1) * 128, :], [], [fst2.b], fst2.b)
        vop(lambda E, a=laA_s[k][:, 128:136], b_=fst2[0:128, 0:8]:
            E.tensor_copy(a, b_), [fst2.b], [laA_s[k].b])
    load_small_bf16(laB_s, laB_d, 128, c.dm)
    dma("Y", rb_s.ap, rb_d.partition_broadcast(128).squeeze(1), [], [rb_s.b], rb_s.b)
    load_small_f32(bb_s, baseb_d)
    load_small_f32(ones_s, ones1_d)
    dma("Y", ident_s.ap, ident_d, [], [ident_s.b], ident_s.b)

    def layer_norm(src, dst_bf, sq_col):
        vop(lambda E, o=stat[:, 0:1], i=src.ap: E.reduce_sum(o, i, axis=AX.X),
            [src.b], [stat.b])
        vop(lambda E, o=stat[:, 1:2], i=stat[:, 0:1]:
            E.tensor_scalar(o, i, 1.0 / c.dm, None, op0=OP.mult),
            [stat.b], [stat.b])
        vop(lambda E, o=trash.ap, i=src.ap, a=stat[:, 2:3]:
            E.activation(o, i, AF.Square, accum_out=a),
            [src.b], [trash.b, stat.b], eng="S")
        vop(lambda E, o=stat[:, 3:4], i=stat[:, 2:3]:
            E.tensor_scalar(o, i, 1.0 / c.dm, None, op0=OP.mult),
            [stat.b], [stat.b])
        vop(lambda E, o=stat[:, 4:5], a=stat[:, 1:2]:
            E.tensor_tensor(o, a, a, op=OP.mult), [stat.b], [stat.b])
        vop(lambda E, o=stat[:, 3:4], a=stat[:, 3:4], b_=stat[:, 4:5]:
            E.tensor_tensor(o, a, b_, op=OP.subtract), [stat.b], [stat.b])
        vop(lambda E, o=stat[:, 5:6], i=stat[:, 3:4], bb=cst[:, 3:4]:
            E.activation(o, i, AF.Ln, bias=bb), [stat.b, cst.b], [stat.b], eng="S")
        vop(lambda E, o=stat[:, 6:7], i=stat[:, 5:6], sc=cst[:, 2:3]:
            E.activation(o, i, AF.Exp, scale=sc), [stat.b, cst.b], [stat.b], eng="S")
        vop(lambda E, o=dst_bf.ap, i=src.ap, m=stat[:, 1:2], r=stat[:, 6:7]:
            E.tensor_scalar(o, i, m, r, op0=OP.subtract, op1=OP.mult),
            [src.b, stat.b], [dst_bf.b])

    def phase_a(b):
        def load_x_a(tq):
            dma("Y", x_sl[tq % 2].ap, x_in[b, tq * 128:(tq + 1) * 128, :],
                [], [x_sl[tq % 2].b], x_sl[tq % 2].b)
        load_x_a(0)
        for tq in range(c.ntq):
            if tq + 1 < c.ntq:
                load_x_a(tq + 1)
            xs = x_sl[tq % 2]
            h1 = h1_sl[tq % 2]
            layer_norm(xs, h1, tq)
            for k in range(c.nkd):
                vop(lambda E, o=PT.ap, i=h1[:, k * 128:(k + 1) * 128], idd=ident_s.ap:
                    E.transpose(o, i, idd), [h1.b, ident_s.b], [PT.b], eng="T")
                col = tq * 128
                a_i = c.tcd * k + col // c.tc
                a_c = col % c.tc
                vop(lambda E, o=hfm[a_i][:, a_c:a_c + 128], i=PT.ap:
                    E.copy(o, i), [PT.b], [hfm[a_i].b], eng="S")
            yield
        nm = 2 * c.ndch
        ntqh = c.L // c.NT
        wt_ring = [wtile[:c.nkd], wtile[c.nkd:]]
        for m in range(nm + 1):
            if m < nm:
                rr = wt_ring[m % 2]
                for k in range(c.nkd):
                    dma("Y", rr[k][:, 0:128],
                        w_inT_bf[k * 128:(k + 1) * 128, m * 128:(m + 1) * 128],
                        [winbf_b], [rr[k].b], rr[k].b)
            if m == 0:
                continue
            mm = m - 1
            rr = wt_ring[mm % 2]
            for half in range(ntqh // 2):
                for k in range(c.nkd):
                    for thh in range(2):
                        th = 2 * half + thh
                        col = th * c.NT
                        a_i = c.tcd * k + col // c.tc
                        a_c = col % c.tc
                        pz = [P[1], P[3]][thh]
                        vop(lambda E, o=pz.ap, w=rr[k][:, 0:128],
                            r_=hfm[a_i][:, a_c:a_c + c.NT], s=(k == 0), p=(k == c.nkd - 1):
                            E.matmul(o, w, r_, start=s, stop=p),
                            [rr[k].b, hfm[a_i].b], [], eng="T", accum=[pz.b])
                for thh in range(2):
                    th = 2 * half + thh
                    pz = [P[1], P[3]][thh]
                    ev = ev_bf[thh]
                    if mm < c.ndch:
                        vop(lambda E, o=ev.ap, i=pz.ap: E.copy(o, i),
                            [pz.b], [ev.b], eng="S")
                        dma("Y", xp_scr[b, mm * 128:(mm + 1) * 128, th * c.NT:(th + 1) * c.NT],
                            ev.ap, [ev.b], [xp_b[b][mm]], ev.b)
                    else:
                        vop(lambda E, o=ev.ap, i=pz.ap:
                            E.activation(o, i, AF.Silu), [pz.b], [ev.b], eng="S")
                        dma("Y", sz_scr[b, (mm - c.ndch) * 128:(mm - c.ndch + 1) * 128,
                                        th * c.NT:(th + 1) * c.NT],
                            ev.ap, [ev.b], [sz_b[b][mm - c.ndch]], ev.b)
            yield

    def phase_m(b, d, filler=None):
        rev = (d == 1)
        tcs = list(range(c.tcd - 1, -1, -1)) if rev else list(range(c.tcd))
        for tci, tcn in enumerate(tcs):
            t0 = tcn * c.tc
            def load_xp(k):
                xs = xp_sl[k % 5]
                if not rev:
                    lo = t0 - 3
                    if lo < 0:
                        vop(lambda E, o=xs[:, 0:3]: E.memset(o, 0.0), [], [xs.b])
                        dma("Y", xs[:, 3:3 + c.tc], xp_scr[b, k * 128:(k + 1) * 128, t0:t0 + c.tc],
                            [xp_b[b][k]], [xs.b], xs.b)
                    else:
                        dma("Y", xs[:, 0:3 + c.tc], xp_scr[b, k * 128:(k + 1) * 128, lo:t0 + c.tc],
                            [xp_b[b][k]], [xs.b], xs.b)
                else:
                    hi = t0 + c.tc + 3
                    if hi > c.L:
                        dma("Y", xs[:, 0:c.tc], xp_scr[b, k * 128:(k + 1) * 128, t0:t0 + c.tc],
                            [xp_b[b][k]], [xs.b], xs.b)
                        vop(lambda E, o=xs[:, c.tc:c.tc + 3]: E.memset(o, 0.0), [], [xs.b])
                    else:
                        dma("Y", xs[:, 0:c.tc + 3], xp_scr[b, k * 128:(k + 1) * 128, t0:hi],
                            [xp_b[b][k]], [xs.b], xs.b)

            for kk in range(min(4, c.ndch)):
                load_xp(kk)
            for k in range(c.ndch):
                if k + 4 < c.ndch:
                    load_xp(k + 4)
                xs = xp_sl[k % 5]
                win = lambda j: xs[:, j:j + c.tc]
                tap = (lambda j: j) if not rev else (lambda j: 3 - j)
                ac = ca[k % 2]
                ceng = "V"
                vop(lambda E, o=ac.ap, i=win(0), w=convw_s[d][:, k * 4 + tap(0):k * 4 + tap(0) + 1]:
                    E.tensor_scalar(o, i, w, None, op0=OP.mult),
                    [xs.b, convw_s[d].b], [ac.b], eng=ceng)
                for j in range(1, 4):
                    vop(lambda E, o=ac.ap, i=win(j),
                        w=convw_s[d][:, k * 4 + tap(j):k * 4 + tap(j) + 1], a=ac.ap:
                        E.scalar_tensor_tensor(o, i, w, a, op0=OP.mult, op1=OP.add),
                        [xs.b, convw_s[d].b, ac.b], [ac.b], eng=ceng)
                ub = ubf[k % 2]
                vop(lambda E, o=ub.ap, i=ac.ap, bb=convb_s[d][:, k:k + 1]:
                    E.activation(o, i, AF.Silu, bias=bb),
                    [ac.b, convb_s[d].b], [ub.b], eng="S")
                dma("G", u_scr[tci % 2, k * 128:(k + 1) * 128, :], ub.ap,
                    [ub.b], [u_b[tci % 2][k]], ub.b)
                for nn in range(c.nnt):
                    vop(lambda E, o=P[2 + nn][0:c.xrows, :], w=wx_s[d][k].ap,
                        r_=ub[:, nn * c.NT:(nn + 1) * c.NT], s=(k == 0), p=(k == c.ndch - 1):
                        E.matmul(o, w, r_, start=s, stop=p),
                        [wx_s[d][k].b, ub.b], [], eng="T", accum=[P[2 + nn].b])
            for nn in range(c.nnt):
                vop(lambda E, o=xdbl[0:c.xrows, nn * c.NT:(nn + 1) * c.NT], i=P[2 + nn][0:c.xrows, :]:
                    E.copy(o, i), [P[2 + nn].b], [xdbl.b], eng="S")
            dma("G", xdbl_scr[c.dtr:c.xrows, :], xdbl[c.dtr:c.xrows, :],
                [xdbl.b], [xdbl_scr_b], xdbl.b)

            use_carry = (tci > 0)
            save_carry = (tci < c.tcd - 1)
            for s in range(c.ds):
                dma("G", arn[s].ap,
                    xdbl_scr[c.dtr + s:c.dtr + s + 1, :].partition_broadcast(128).squeeze(1),
                    [xdbl_scr_b], [arn[s].b], arn[s].b)
                dma("G", arn[16 + s].ap,
                    xdbl_scr[c.dtr + c.ds + s:c.dtr + c.ds + s + 1, :].partition_broadcast(128).squeeze(1),
                    [xdbl_scr_b], [arn[16 + s].b], arn[16 + s].b)

            def chain_mm(k):
                if k >= c.ndch:
                    return
                us = u_sl[k % 3]
                dma("Y", us.ap, u_scr[tci % 2, k * 128:(k + 1) * 128, :],
                    [u_b[tci % 2][k]], [us.b], us.b)
                for nn in range(c.nnt):
                    vop(lambda E, o=P[4 + (k % 2)].ap, w=wdt_s[d][:, k * 128:(k + 1) * 128],
                        r_=xdbl[0:c.dtr, nn * c.NT:(nn + 1) * c.NT]:
                        E.matmul(o, w, r_, start=True, stop=True),
                        [wdt_s[d].b, xdbl.b], [], eng="T", accum=[P[4 + (k % 2)].b])

            def chain_ops(k):
                if k >= c.ndch:
                    return []
                us = u_sl[k % 3]
                dt_ = dtb[k % 2]
                sA = chA[k % 2]
                sB = chB[k % 2]
                ops = []
                for nn in range(c.nnt):
                    sl = slice(nn * c.NT, (nn + 1) * c.NT)
                    ops.append(lambda sl=sl: vop(
                        lambda E, o=sB[:, sl], i=P[4 + (k % 2)].ap, bb=bdt_s[d][:, k:k + 1]:
                        E.activation(o, i, AF.Identity, bias=bb),
                        [P[4 + (k % 2)].b, bdt_s[d].b], [sB.b], eng="S"))
                    ops.append(lambda sl=sl: vop(
                        lambda E, o=sA[:, sl], i=sB[:, sl]:
                        E.activation(o, i, AF.Abs), [sB.b], [sA.b], eng="S"))
                    ops.append(lambda sl=sl: vop(
                        lambda E, o=sA[:, sl], i=sA[:, sl], sc=cst[:, 1:2]:
                        E.activation(o, i, AF.Exp, scale=sc),
                        [sA.b, cst.b], [sA.b], eng="S"))
                    ops.append(lambda sl=sl: vop(
                        lambda E, o=sA[:, sl], i=sA[:, sl], bb=cst[:, 0:1]:
                        E.activation(o, i, AF.Ln, bias=bb),
                        [sA.b, cst.b], [sA.b], eng="S"))
                    ops.append(lambda sl=sl: vop(
                        lambda E, o=sB[:, sl], i=sB[:, sl]:
                        E.activation(o, i, AF.Relu), [sB.b], [sB.b], eng="S"))
                    ops.append(lambda sl=sl: vop(
                        lambda E, o=dt_[:, sl], a=sA[:, sl], b_=sB[:, sl]:
                        E.tensor_tensor(o, a, b_, op=OP.add),
                        [sA.b, sB.b], [dt_.b]))
                ops.append(lambda: vop(
                    lambda E, o=dtu[k % 2].ap, a=dt_.ap, b_=us.ap:
                    E.tensor_tensor(o, a, b_, op=OP.mult), [dt_.b, us.b], [dtu[k % 2].b]))
                return ops

            def emit_dA(k, s, gi):
                da = dA_t[gi % 3]
                vop(lambda E, o=da.ap, i=dtb[k % 2].ap,
                    sc=negA_s[d][:, k * c.ds + s:k * c.ds + s + 1]:
                    E.activation(o, i, AF.Exp, scale=sc),
                    [dtb[k % 2].b, negA_s[d].b], [da.b], eng="S")

            def emit_dtBu(k, s, gi):
                db = dBu[gi % 3]
                vop(lambda E, o=db.ap, a=dtu[k % 2].ap, b_=arn[s].ap:
                    E.tensor_tensor(o, a, b_, op=OP.mult),
                    [dtu[k % 2].b, arn[s].b], [db.b])

            def emit_scan(k, s, gi):
                da = dA_t[gi % 3]
                db = dBu[gi % 3]
                h = hh[gi % 3]
                seng = "V"
                if rev:
                    if use_carry:
                        vop(lambda E, o=h[:, c.tc - 1::-1], a=da[:, c.tc - 1::-1],
                            b_=db[:, c.tc - 1::-1], ini=carry[k][:, s:s + 1]:
                            E.tensor_tensor_scan(o, a, b_, ini, op0=OP.mult, op1=OP.add),
                            [da.b, db.b, carry[k].b], [h.b], eng=seng)
                    else:
                        vop(lambda E, o=h[:, c.tc - 1::-1], a=da[:, c.tc - 1::-1],
                            b_=db[:, c.tc - 1::-1]:
                            E.tensor_tensor_scan(o, a, b_, 0.0, op0=OP.mult, op1=OP.add),
                            [da.b, db.b], [h.b], eng=seng)
                else:
                    if use_carry:
                        vop(lambda E, o=h.ap, a=da.ap, b_=db.ap, ini=carry[k][:, s:s + 1]:
                            E.tensor_tensor_scan(o, a, b_, ini, op0=OP.mult, op1=OP.add),
                            [da.b, db.b, carry[k].b], [h.b], eng=seng)
                    else:
                        vop(lambda E, o=h.ap, a=da.ap, b_=db.ap:
                            E.tensor_tensor_scan(o, a, b_, 0.0, op0=OP.mult, op1=OP.add),
                            [da.b, db.b], [h.b], eng=seng)
                if save_carry:
                    col = 0 if rev else c.tc - 1
                    vop(lambda E, o=carry[k][:, s:s + 1], i=h[:, col:col + 1]:
                        E.copy(o, i), [h.b], [carry[k].b], eng="S")

            def emit_cmult_y(k, s, gi):
                h = hh[gi % 3]
                hc = hcc[gi % 2]
                vop(lambda E, o=hc.ap, a=h.ap, b_=arn[16 + s].ap:
                    E.tensor_tensor(o, a, b_, op=OP.mult),
                    [h.b, arn[16 + s].b], [hc.b])
                py = P[0] if k % 2 == 0 else P[2]
                for nn in range(c.nnt):
                    vop(lambda E, o=py.ap, w=ident_s.ap,
                        r_=hc[:, nn * c.NT:(nn + 1) * c.NT],
                        s_=(s == 0), p=(s == c.ds - 1):
                        E.matmul(o, w, r_, start=s_, stop=p),
                        [ident_s.b, hc.b], [], eng="T", accum=[py.b])

            sz_hold = {}

            def epilogue(k):
                us = u_sl[k % 3]
                t1 = acc_f[k % 2]
                py = P[0] if k % 2 == 0 else P[2]
                for nn in range(c.nnt):
                    vop(lambda E, o=t1[:, nn * c.NT:(nn + 1) * c.NT],
                        i=us[:, nn * c.NT:(nn + 1) * c.NT],
                        dd=Dd_s[d][:, k:k + 1], p_=py.ap:
                        E.scalar_tensor_tensor(o, i, dd, p_, op0=OP.mult, op1=OP.add),
                        [us.b, Dd_s[d].b, py.b], [t1.b])
                vop(lambda E, o=yg[k].ap, a=t1.ap, b_=ss_hold[k].ap:
                    E.tensor_tensor(o, a, b_, op=OP.mult),
                    [t1.b, ss_hold[k].b], [yg[k].b])

            chain_mm(0)
            for f in chain_ops(0):
                f()
            ss_hold = {}
            nxt = []
            ni = 0
            total = c.ndch * c.ds
            for gi in range(total + 2):
                k, s = divmod(gi, c.ds)
                if s == 0 and k < c.ndch:
                    if filler is not None:
                        filler()
                    ss = sz_sl[k % 2]
                    ss_hold[k] = ss
                    dma("Y", ss.ap, sz_scr[b, k * 128:(k + 1) * 128, t0:t0 + c.tc],
                        [sz_b[b][k]], [ss.b], ss.b)
                    while ni < len(nxt):
                        nxt[ni]()
                        ni += 1
                    chain_mm(k + 1)
                    nxt = chain_ops(k + 1)
                    ni = 0
                if gi < total:
                    emit_dA(k, s, gi)
                    emit_dtBu(k, s, gi)
                if 2 <= s and ni < len(nxt):
                    want = min(len(nxt), (s - 1) * len(nxt) // 6)
                    while ni <= want and ni < len(nxt):
                        nxt[ni]()
                        ni += 1
                if gi >= 1 and gi - 1 < total:
                    k1, s1 = divmod(gi - 1, c.ds)
                    emit_scan(k1, s1, gi - 1)
                if gi >= 2:
                    k2, s2 = divmod(gi - 2, c.ds)
                    emit_cmult_y(k2, s2, gi - 2)
                if s == 3 and k >= 1:
                    epilogue(k - 1)
            epilogue(c.ndch - 1)

            tqs = list(range(c.ntq_tc))
            blocks = [tqs[i:i + 3] for i in range(0, len(tqs), 3)]
            for blk_tqs in blocks:
                def load_wout(k):
                    ws = wout_sl[k % 4]
                    dma("Y", ws.ap, w_outT_bf[k * 128:(k + 1) * 128, :],
                        [woutbf_b], [ws.b], ws.b)
                for kk in range(min(3, c.ndch)):
                    load_wout(kk)
                for k in range(c.ndch):
                    if k + 3 < c.ndch:
                        load_wout(k + 3)
                    ws = wout_sl[k % 4]
                    for ti, tq in enumerate(blk_tqs):
                        for hj in range(c.dm // c.NT):
                            vop(lambda E, o=P[2 * ti + hj].ap,
                                w=yg[k][:, tq * 128:(tq + 1) * 128],
                                r_=ws[:, hj * c.NT:(hj + 1) * c.NT],
                                s_=(k == 0), p=(k == c.ndch - 1):
                                E.matmul(o, w, r_, start=s_, stop=p),
                                [yg[k].b, ws.b], [], eng="T", accum=[P[2 * ti + hj].b])
                for ti, tq in enumerate(blk_tqs):
                    row = t0 + tq * 128
                    gq = row // 128
                    ps = pstg[0]
                    if not rev:
                        for hj in range(c.dm // c.NT):
                            vop(lambda E, o=ps[:, hj * c.NT:(hj + 1) * c.NT],
                                i=P[2 * ti + hj].ap:
                                E.copy(o, i), [P[2 * ti + hj].b], [ps.b], eng="S")
                    else:
                        pf = pfld[ti % 2]
                        dma("Y", pf.ap, p_scr[b, row:row + 128, :], [p_b[b][gq]], [pf.b], pf.b)
                        for hj in range(c.dm // c.NT):
                            vop(lambda E, o=ps[:, hj * c.NT:(hj + 1) * c.NT],
                                i=pf[:, hj * c.NT:(hj + 1) * c.NT], p_=P[2 * ti + hj].ap:
                                E.scalar_tensor_tensor(o, i, 1.0, p_, op0=OP.mult, op1=OP.add),
                                [pf.b, P[2 * ti + hj].b], [ps.b])
                    dma("Y", p_scr[b, row:row + 128, :], ps.ap,
                        [ps.b], [p_b[b][gq]], ps.b)

    def phase_f(b):
        def load_xp_f(tq):
            dma("Y", x_sl[tq % 2].ap, x_in[b, tq * 128:(tq + 1) * 128, :],
                [], [x_sl[tq % 2].b], x_sl[tq % 2].b)
            dma("Y", pfld[tq % 2].ap, p_scr[b, tq * 128:(tq + 1) * 128, :],
                [p_b[b][tq]], [pfld[tq % 2].b], pfld[tq % 2].b)
        load_xp_f(0)
        for tq in range(c.ntq):
            if tq + 1 < c.ntq:
                load_xp_f(tq + 1)
            xs = x_sl[tq % 2]
            pf = pfld[tq % 2]
            o1 = out1t[tq % 2]
            h2 = h2_sl[tq % 2]
            vop(lambda E, o=o1.ap, a=xs.ap, b_=pf.ap:
                E.tensor_tensor(o, a, b_, op=OP.add), [xs.b, pf.b], [o1.b])
            layer_norm(o1, h2, tq)
            for k in range(c.nkd):
                vop(lambda E, o=PT.ap, i=h2[:, k * 128:(k + 1) * 128], idd=ident_s.ap:
                    E.transpose(o, i, idd), [h2.b, ident_s.b], [PT.b], eng="T")
                vop(lambda E, o=h2T[k].ap, i=PT.ap: E.copy(o, i),
                    [PT.b], [h2T[k].b], eng="S")
            pm = [P[0], P[1]] if tq % 2 == 0 else [P[2], P[3]]
            pax = PAX
            nh = c.dm // c.NT
            for k in range(c.nkd):
                bw = bw_sl[k % 2]
                dma("Y", bw.ap, basewT_bf[k * 128:(k + 1) * 128, :],
                    [basebf_b], [bw.b], bw.b)
                for hj in range(nh):
                    vop(lambda E, o=pm[hj].ap, w=h2T[k].ap,
                        r_=bw[:, hj * c.NT:(hj + 1) * c.NT], s_=(k == 0):
                        E.matmul(o, w, r_, start=s_, stop=False),
                        [h2T[k].b, bw.b], [], eng="T", accum=[pm[hj].b])
                vop(lambda E, o=pax.ap, w=h2T[k].ap, r_=laA_s[k].ap, s_=(k == 0),
                    p=(k == c.nkd - 1):
                    E.matmul(o, w, r_, start=s_, stop=p),
                    [h2T[k].b, laA_s[k].b], [], eng="T", accum=[pax.b])
            for hj in range(nh):
                vop(lambda E, o=pm[hj].ap, w=ones_s.ap,
                    r_=bb_s[0:1, hj * c.NT:(hj + 1) * c.NT]:
                    E.matmul(o, w, r_, start=False, stop=False),
                    [ones_s.b, bb_s.b], [], eng="T", accum=[pm[hj].b])
            vop(lambda E, o=l8.ap, i=pax[:, 128:136], r_=rb_s.ap:
                E.tensor_tensor(o, i, r_, op=OP.add), [pax.b, rb_s.b], [l8.b])
            vop(lambda E, o=st2[:, 0:1], i=l8.ap: E.reduce_max(o, i, axis=AX.X),
                [l8.b], [st2.b])
            vop(lambda E, o=e8.ap, i=l8.ap, m=st2[:, 0:1]:
                E.tensor_scalar(o, i, m, None, op0=OP.is_ge), [l8.b, st2.b], [e8.b])
            vop(lambda E, o=st2[:, 1:2], i=e8.ap: E.reduce_sum(o, i, axis=AX.X),
                [e8.b], [st2.b])
            vop(lambda E, o=e8.ap, i=e8.ap, l=l8.ap:
                E.scalar_tensor_tensor(o, i, -1e30, l, op0=OP.mult, op1=OP.add),
                [e8.b, l8.b], [e8.b])
            vop(lambda E, o=st2[:, 2:3], i=e8.ap: E.reduce_max(o, i, axis=AX.X),
                [e8.b], [st2.b])
            vop(lambda E, o=st2[:, 3:4], i=st2[:, 1:2]:
                E.tensor_scalar(o, i, 2.0, None, op0=OP.is_ge), [st2.b], [st2.b])
            vop(lambda E, o=st2[:, 4:5], a=st2[:, 0:1], b_=st2[:, 2:3]:
                E.tensor_tensor(o, a, b_, op=OP.subtract), [st2.b], [st2.b])
            vop(lambda E, o=st2[:, 4:5], a=st2[:, 4:5], b_=st2[:, 3:4]:
                E.tensor_tensor(o, a, b_, op=OP.mult), [st2.b], [st2.b])
            vop(lambda E, o=st2[:, 4:5], a=st2[:, 4:5], b_=st2[:, 2:3]:
                E.tensor_tensor(o, a, b_, op=OP.add), [st2.b], [st2.b])
            vop(lambda E, o=e8.ap, i=l8.ap, kk=st2[:, 4:5]:
                E.tensor_scalar(o, i, kk, None, op0=OP.is_lt), [l8.b, st2.b], [e8.b])
            vop(lambda E, o=l8.ap, i=e8.ap, l=l8.ap:
                E.scalar_tensor_tensor(o, i, -1e9, l, op0=OP.mult, op1=OP.add),
                [e8.b, l8.b], [l8.b])
            vop(lambda E, o=l8.ap, i=l8.ap, m=st2[:, 0:1]:
                E.tensor_scalar(o, i, m, None, op0=OP.subtract), [l8.b, st2.b], [l8.b])
            vop(lambda E, o=e8.ap, i=l8.ap: E.activation(o, i, AF.Exp),
                [l8.b], [e8.b], eng="S")
            vop(lambda E, o=st2[:, 5:6], i=e8.ap: E.reduce_sum(o, i, axis=AX.X),
                [e8.b], [st2.b])
            vop(lambda E, o=st2[:, 6:7], i=st2[:, 5:6]: E.reciprocal(o, i),
                [st2.b], [st2.b])
            vop(lambda E, o=w2b.ap, i=e8.ap, r_=st2[:, 6:7]:
                E.tensor_scalar(o, i, r_, 2.0, op0=OP.mult, op1=OP.mult),
                [e8.b, st2.b], [w2b.b])
            vop(lambda E, o=axb.ap, i=pax[:, 0:128]: E.tensor_copy(o, i),
                [pax.b], [axb.b])
            vop(lambda E, o=axw.ap.rearrange("p (e r) -> p e r", r=c.ds),
                a=axb.ap.rearrange("p (e r) -> p e r", r=c.ds),
                b_=w2b.ap.unsqueeze(2).broadcast_to((128, 8, c.ds)):
                E.tensor_tensor(o, a, b_, op=OP.mult),
                [axb.b, w2b.b], [axw.b])
            vop(lambda E, o=PT.ap, i=axw.ap, idd=ident_s.ap:
                E.transpose(o, i, idd), [axw.b, ident_s.b], [PT.b], eng="T")
            vop(lambda E, o=axwT.ap, i=PT.ap: E.copy(o, i), [PT.b], [axwT.b], eng="S")
            for hj in range(nh):
                vop(lambda E, o=pm[hj].ap, w=axwT.ap,
                    r_=laB_s[:, hj * c.NT:(hj + 1) * c.NT]:
                    E.matmul(o, w, r_, start=False, stop=True),
                    [axwT.b, laB_s.b], [], eng="T", accum=[pm[hj].b])
            for hj in range(nh):
                os = outst[hj % 2]
                vop(lambda E, o=os.ap, i=o1[:, hj * c.NT:(hj + 1) * c.NT], p_=pm[hj].ap:
                    E.scalar_tensor_tensor(o, i, 1.0, p_, op0=OP.mult, op1=OP.add),
                    [o1.b, pm[hj].b], [os.b])
                dma("Y", out_d[b, tq * 128:(tq + 1) * 128, hj * c.NT:(hj + 1) * c.NT],
                    os.ap, [os.b], [], os.b)
            yield

    a_gens = [phase_a(b) for b in range(c.nb)]
    f_gens = [phase_f(b) for b in range(c.nb)]
    for _ in a_gens[0]:
        pass
    for b in range(c.nb):
        nxt = a_gens[b + 1] if b + 1 < c.nb else None
        fprev = f_gens[b - 1] if b >= 1 else None

        def filler(n=1, ga=nxt, gf=fprev):
            for g in (gf, ga):
                if g is not None and next(g, "DONE") != "DONE":
                    return
        phase_m(b, 0, filler)
        phase_m(b, 1, filler)
        if nxt is not None:
            for _ in nxt:
                pass
        if fprev is not None:
            for _ in fprev:
                pass
    for _ in f_gens[c.nb - 1]:
        pass

    pl.barrier("Y", [outst[0].b, outst[1].b])
    pl.barrier("G", [outst[0].b, outst[1].b])

    pl.finalize()
    eng_sems = {}
    dma_sems = {}
    with stk:
        for e in ("T", "V", "S", "G"):
            eng_sems[e] = stk.enter_context(nc.semaphore(f"es_{e}"))
        for key in pl.dma_count:
            dma_sems[key] = stk.enter_context(nc.semaphore(f"ds_{key[1]}"))

        def emit_engine(E, e):
            for o in pl.eng_ops[e]:
                for (sk, val) in o.waits:
                    h = eng_sems[sk] if sk in pl.eng_ops else dma_sems[sk]
                    E.wait_ge(h, val)
                if o.fn is None:
                    continue
                inst = o.fn(E)
                if o.dma_key is not None:
                    inst.then_inc(dma_sems[o.dma_key], 16)
                elif o.inc:
                    inst.then_inc(eng_sems[e], 1)

        with nc.Block() as blk:
            @blk.tensor
            def _(E):
                emit_engine(E, "T")

            @blk.vector
            def _(E):
                emit_engine(E, "V")

            @blk.scalar
            def _(E):
                emit_engine(E, "S")

            @blk.gpsimd
            def _(E):
                emit_engine(E, "G")

            @blk.sync
            def _(E):
                emit_engine(E, "Y")

    return nc




def prep_inputs(cfg, full):
    c = cfg
    x = np.ascontiguousarray(full["L0_attn_hidden_states"])
    f32 = np.float32

    def t(a):
        return np.ascontiguousarray(np.asarray(a).T)

    def chunk_cols(a, width=128):
        a = np.asarray(a)
        if a.ndim == 1:
            return np.ascontiguousarray(a.reshape(-1, width).T)
        n = a.shape[0] // width
        return np.ascontiguousarray(
            a.reshape(n, width, a.shape[1]).transpose(1, 0, 2).reshape(width, -1))

    shared = {
        "w_inT": t(full["L0_attn_W_in"]).astype(f32),
        "w_outT": t(full["L0_attn_W_out"]).astype(f32),
        "basewT": t(full["L0_ffn_base_w"]).astype(f32),
        "baseb": np.asarray(full["L0_ffn_base_b"], f32).reshape(1, -1),
        "laA": np.ascontiguousarray(
            np.asarray(full["L0_ffn_lora_A"], f32).transpose(1, 0, 2).reshape(c.dm, -1)),
        "laB": np.ascontiguousarray(
            np.asarray(full["L0_ffn_lora_B"], f32).reshape(-1, c.dm)),
        "rw": t(full["L0_ffn_router_w"]).astype(f32),
        "rb": np.asarray(full["L0_ffn_router_b"], f32).reshape(1, -1),
        "ident": np.eye(128).astype(BFNP),
        "ones1": np.ones((1, 128), f32),
    }
    for d, sfx in enumerate(("f", "r")):
        shared[f"wxT_{d}"] = t(full[f"L0_attn_W_x_{sfx}"]).astype(f32)
        shared[f"wdtT_{d}"] = t(full[f"L0_attn_W_dt_{sfx}"]).astype(f32)
        shared[f"convw_{d}"] = np.ascontiguousarray(
            np.asarray(full[f"L0_attn_conv_w_{sfx}"], f32)
            .reshape(c.ndch, 128, 4).transpose(1, 0, 2).reshape(128, -1))
        shared[f"convb_{d}"] = chunk_cols(np.asarray(full[f"L0_attn_conv_b_{sfx}"], f32))
        shared[f"bdt_{d}"] = chunk_cols(np.asarray(full[f"L0_attn_b_dt_{sfx}"], f32))
        shared[f"Dvec_{d}"] = chunk_cols(np.asarray(full[f"L0_attn_D_{sfx}"], f32))
        shared[f"alog_{d}"] = np.ascontiguousarray(
            np.asarray(full[f"L0_attn_A_log_{sfx}"], f32)
            .reshape(c.ndch, 128, c.ds).transpose(1, 0, 2).reshape(128, -1))

    in_maps = []
    for i in range(N_CORES):
        m = dict(shared)
        m["x"] = np.ascontiguousarray(x[i * c.nb:(i + 1) * c.nb], f32)
        in_maps.append(m)
    return in_maps


def kernel(**inputs):
    cfg = CFG(dm=1024, L=2048, nb=32 // N_CORES)
    nc = build(cfg)
    in_maps = prep_inputs(cfg, inputs)
    res = run_bass_kernel_spmd(nc, in_maps, core_ids=list(range(N_CORES)))
    out = np.concatenate([np.asarray(r["out"], np.float32) for r in res.results], axis=0)
    return np.ascontiguousarray(out)
